# version 13
# baseline (speedup 1.0000x reference)
import contextlib

import numpy as np
import ml_dtypes
import concourse.bass as bass
import concourse.mybir as mybir
from concourse.bass_utils import run_bass_kernel_spmd

F32 = mybir.dt.float32
BF16 = mybir.dt.bfloat16
AF = mybir.ActivationFunctionType
ALU = mybir.AluOpType
AX = mybir.AxisListType.X
bfdt = ml_dtypes.bfloat16

D = 2048
LBATCH = 4096
TOWN = 1024
NH = 32
NKV = 8
HD = 64
NE = 8
HF = 1408
NDC = 16
NHC = 11
NTL = 32
NTO = 8

EXP_BIAS = -16.0
LN_EPS = 1e-5
RECIP_D = 1.0 / 2048.0
BIG = 1e30

ENGS = ("sync", "gpsimd", "vector", "scalar", "tensor")


DMA_SEMS = ("s_xp", "s_x0", "s_x1", "s_w0", "s_w1", "s_w2", "s_w3",
            "s_y", "s_ca", "s_cb", "s_cc")


class Pl:

    def __init__(self, sems):
        self.q = {e: [] for e in ENGS}
        self.sems = sems
        self.cnt = {k: 0 for k in sems}
        self.last_wait = {e: {k: 0 for k in sems} for e in ENGS}
        self.xp_pending = 0

    def wait(self, eng, sem, val):
        if val is None or val <= 0 or self.last_wait[eng][sem] >= val:
            return
        self.last_wait[eng][sem] = val
        h = self.sems[sem]
        self.q[eng].append(lambda e, h=h, v=val: e.wait_ge(h, v))

    def wait_t(self, eng, tok):
        if tok:
            self.wait(eng, tok[0], tok[1])

    def op(self, eng, fn, inc=None, n=1):
        if eng == "vector":
            self.wait(eng, "s_dve", self.cnt["s_dve"])
            inc, n = "s_dve", 1
        elif eng == "scalar":
            self.wait(eng, "s_act", self.cnt["s_act"])
            inc, n = "s_act", 1
        if inc is None:
            self.q[eng].append(fn)
            return None
        h = self.sems[inc]
        self.cnt[inc] += n
        self.q[eng].append(lambda e, fn=fn, h=h, n=n: fn(e).then_inc(h, n))
        return self.cnt[inc]

    def dma(self, out_ap, in_ap, sem="s_y"):
        if sem == "s_y":
            self.wait("sync", "s_y", self.cnt["s_y"])
        v = self.op("sync", lambda e, o=out_ap, i=in_ap: e.dma_start(o, i),
                    inc=sem, n=16)
        return (sem, v)

    def xp(self, out_ap, in_ap):
        self.wait("sync", "s_xp", self.cnt["s_xp"])
        v = self.op("sync", lambda e, o=out_ap, i=in_ap:
                    e.dma_start(o, i, transpose=True), inc="s_xp", n=16)
        self.xp_pending = v
        return ("s_xp", v)

    def gdma(self, fn, sem="s_ca"):
        self.wait("gpsimd", sem, self.cnt[sem])
        return (sem, self.op("gpsimd", fn, inc=sem, n=16))

    def barrier(self, engines=ENGS, exclude=()):
        tot = dict(self.cnt)
        for e in engines:
            for s, v in tot.items():
                if s in exclude:
                    continue
                self.wait(e, s, v)


class WStream:

    def __init__(self, p, buf, nslots, conv, extra_f32=(), nsem=2):
        self.p = p
        n = nslots
        flat = buf[:, :, :].rearrange("p a b -> p (a b)")
        half = n * 2048 // 2
        self.bf = [flat[:, i * 2048:(i + 1) * 2048] for i in range(n // 2)]
        f32v = flat[:, half:2 * half].bitcast(F32)
        self.f32 = [f32v[:, i * 2048:(i + 1) * 2048] for i in range(n // 4)] \
            if n >= 4 else [f32v[:, 0:1024]]
        self.f32 += list(extra_f32)
        self.nsem = nsem
        self.conv = conv
        self.csem = "s_act" if conv == "scalar" else "s_dve"
        self.i = 0
        self.conv_hist = {}
        self.xp_hist = {}
        self.srcs = None
        self.ld_hist = {}

    def _load(self, j):
        src, cols, src2 = self.srcs[j]
        fs = self.f32[j % len(self.f32)]
        dsem = "s_w%d" % (j % self.nsem)
        if src2 is None:
            ld = self.p.dma(fs[:, 0:cols], src, sem=dsem)
        else:
            self.p.dma(fs[0:64, 0:cols], src, sem=dsem)
            ld = self.p.dma(fs[64:128, 0:cols], src2, sem=dsem)
        self.ld_hist[j] = ld

    def chunk2(self, dst_ap, war_val=0):
        p = self.p
        i = self.i
        self.i += 1
        nf, nb = len(self.f32), len(self.bf)
        if i == 0:
            for j in range(min(nf, len(self.srcs))):
                self._load(j)
        cols = self.srcs[i][1]
        fs, bs = self.f32[i % nf], self.bf[i % nb]
        p.wait_t(self.conv, self.ld_hist[i])
        if i >= nb:
            p.wait_t(self.conv, self.xp_hist[i - nb])
        cv = p.op(self.conv,
                  lambda v, o=bs[:, 0:cols], i_=fs[:, 0:cols]:
                  v.tensor_copy(o, i_), inc=self.csem)
        self.conv_hist[i] = cv
        p.wait("sync", self.csem, cv)
        if i + nf < len(self.srcs):
            self._load(i + nf)
        p.wait("sync", "s_pe", war_val)
        xv = p.xp(dst_ap, bs[:, 0:cols])
        self.xp_hist[i] = xv
        return xv

    def chunk(self, src_ap, dst_ap, cols, war_val=0, src2=None):
        p = self.p
        i = self.i
        self.i += 1
        nf, nb = len(self.f32), len(self.bf)
        fs, bs = self.f32[i % nf], self.bf[i % nb]
        if i >= nf:
            p.wait("sync", self.csem, self.conv_hist[i - nf])
        dsem = "s_w%d" % (i % self.nsem)
        if src2 is None:
            ld = p.dma(fs[:, 0:cols], src_ap, sem=dsem)
        else:
            p.dma(fs[0:64, 0:cols], src_ap, sem=dsem)
            ld = p.dma(fs[64:128, 0:cols], src2, sem=dsem)
        p.wait_t(self.conv, ld)
        if i >= nb:
            p.wait_t(self.conv, self.xp_hist[i - nb])
        if self.conv == "scalar":
            cv = p.op("scalar", lambda s, o=bs[:, 0:cols], i_=fs[:, 0:cols]:
                      s.copy(o, i_), inc="s_act")
        else:
            cv = p.op("vector", lambda v, o=bs[:, 0:cols], i_=fs[:, 0:cols]:
                      v.tensor_copy(o, i_), inc="s_dve")
        self.conv_hist[i] = cv
        p.wait("sync", self.csem, cv)
        p.wait("sync", "s_pe", war_val)
        xv = p.xp(dst_ap, bs[:, 0:cols])
        self.xp_hist[i] = xv
        return xv


def build_nc():
    nc = bass.Bass("TRN2", target_bir_lowering=False)

    xb = nc.dram_tensor("xb", [LBATCH, D], F32, kind="ExternalInput").ap()
    wq = nc.dram_tensor("wq", [D, D], F32, kind="ExternalInput").ap()
    wk = nc.dram_tensor("wk", [512, D], F32, kind="ExternalInput").ap()
    wv = nc.dram_tensor("wv", [512, D], F32, kind="ExternalInput").ap()
    wo = nc.dram_tensor("wo", [D, D], F32, kind="ExternalInput").ap()
    wg = nc.dram_tensor("wg", [NE, D], F32, kind="ExternalInput").ap()
    w1 = nc.dram_tensor("w1", [NE, HF, D], F32, kind="ExternalInput").ap()
    w2 = nc.dram_tensor("w2", [NE, HF, D], F32, kind="ExternalInput").ap()
    w3 = nc.dram_tensor("w3", [NE, D, HF], F32, kind="ExternalInput").ap()
    cos_t = nc.dram_tensor("cos_t", [128, NTL, HD], BF16, kind="ExternalInput").ap()
    sin_t = nc.dram_tensor("sin_t", [128, NTL, HD], BF16, kind="ExternalInput").ap()
    triu_c = nc.dram_tensor("triu_c", [128, 128], BF16, kind="ExternalInput").ap()
    route_c = nc.dram_tensor("route_c", [128, 16], F32, kind="ExternalInput").ap()
    tok_c = nc.dram_tensor("tok_c", [128, 16], mybir.dt.int32, kind="ExternalInput").ap()
    out = nc.dram_tensor("out", [TOWN, D], F32, kind="ExternalOutput").ap()
    xn_scr = nc.dram_tensor("xn_scr", [1152, D], BF16, kind="Internal").ap()
    w_scr = nc.dram_tensor("w_scr", [1152, NE], F32, kind="Internal").ap()
    elist = nc.dram_tensor("elist", [NE * 640, 2], mybir.dt.int32, kind="Internal").ap()
    y_scr = nc.dram_tensor("y_scr", [1152, D], BF16, kind="Internal").ap()
    CAP = 384

    ctx = contextlib.ExitStack()
    with ctx:
        sems = {}
        for name in DMA_SEMS + ("s_pe", "s_act", "s_dve"):
            sems[name] = ctx.enter_context(nc.semaphore(name))
        p = Pl(sems)

        consts = ctx.enter_context(nc.sbuf_tensor("consts", [128, 16], F32))
        xn_own = ctx.enter_context(nc.sbuf_tensor("xn_own", [128, NDC, TOWN], BF16))
        g8 = ctx.enter_context(nc.sbuf_tensor("g8", [128, NTO, NE], F32))
        wgate = ctx.enter_context(nc.sbuf_tensor("wgate", [128, NTO, NE], F32))
        wgt = ctx.enter_context(nc.sbuf_tensor("wgt", [128, NDC, 16], BF16))
        triu_s = ctx.enter_context(nc.sbuf_tensor("triu_s", [128, 128], BF16))
        route_s = ctx.enter_context(nc.sbuf_tensor("route_s", [128, 16], F32))
        tok_s = ctx.enter_context(nc.sbuf_tensor("tok_s", [128, 16], mybir.dt.int32))
        selm = ctx.enter_context(nc.sbuf_tensor("selm", [128, NTO, NE], BF16))
        ridx = ctx.enter_context(nc.sbuf_tensor("ridx", [128, NTO, NE], F32))
        ridxu = ctx.enter_context(nc.sbuf_tensor("ridxu", [128, NTO, NE], mybir.dt.uint32))
        rtot = ctx.enter_context(nc.sbuf_tensor("rtot", [128, 80], F32))
        ones128r = ctx.enter_context(nc.sbuf_tensor("ones128r", [1, 128], F32))
        ones128c = ctx.enter_context(nc.sbuf_tensor("ones128c", [128, 1], BF16))
        pfill = ctx.enter_context(nc.sbuf_tensor("pfill", [128, 80], mybir.dt.int32))
        sA = ctx.enter_context(nc.sbuf_tensor("sA", [128, NDC, 512], BF16))
        sB = ctx.enter_context(nc.sbuf_tensor("sB", [128, NDC, 512], BF16))
        pbig = ctx.enter_context(nc.psum_tensor("pbig", [128, 4096], F32))

        def bank(i, w=512):
            return pbig[:, i * 512:i * 512 + w]

        rstack = contextlib.ExitStack()
        K_rt = rstack.enter_context(nc.sbuf_tensor("K_rt", [128, 4, LBATCH], BF16, side="right"))
        V_aug = rstack.enter_context(nc.sbuf_tensor("V_aug", [128, NTL, 520], BF16, side="right"))

        astack2 = contextlib.ExitStack()
        cos_s = astack2.enter_context(nc.sbuf_tensor("cos_s", [128, NTL, HD], BF16))
        sin_s = astack2.enter_context(nc.sbuf_tensor("sin_s", [128, NTL, HD], BF16))
        scrA = astack2.enter_context(nc.sbuf_tensor("scrA", [128, 512], F32))
        scrB = astack2.enter_context(nc.sbuf_tensor("scrB", [128, 512], F32))
        stats = astack2.enter_context(nc.sbuf_tensor("stats", [128, 8], F32))
        Q_tm = astack2.enter_context(nc.sbuf_tensor("Q_tm", [128, 2, 512], BF16))
        bigA = astack2.enter_context(nc.sbuf_tensor("bigA", [128, 4, D], BF16))
        astack1 = contextlib.ExitStack()
        xbuf = astack1.enter_context(nc.sbuf_tensor("xbuf", [128, 2, D], F32))
        xn_tm = astack1.enter_context(nc.sbuf_tensor("xn_tm", [128, 2, D], BF16))
        xnf = astack1.enter_context(nc.sbuf_tensor("xnf", [128, 3, NDC, 128], BF16))
        K_tm = astack1.enter_context(nc.sbuf_tensor("K_tm", [128, 4, 512], BF16))

        cvals = [0.125, EXP_BIAS, LN_EPS, RECIP_D, 1.0, -1.0, BIG, -BIG, 0.5]
        for i, v in enumerate(cvals):
            nc.const_aps.register(F32, v, consts[:, i:i + 1])

        def vmemsets(v):
            for i, val in enumerate(cvals):
                v.memset(consts[:, i:i + 1], val)
            return v.memset(
                V_aug[:, :, :].rearrange("p l (h c) -> p l h c", c=65)[:, :, :, 64:65],
                1.0)
        p.op("vector", vmemsets, inc="s_dve")
        for e in ("scalar", "tensor", "sync"):
            p.wait(e, "s_dve", 1)

        p.dma(cos_s[:, :, :], cos_t, sem="s_y")
        v_tab = p.dma(sin_s[:, :, :], sin_t, sem="s_y")
        p.wait_t("vector", v_tab)
        p.dma(triu_s[:, :], triu_c, sem="s_y")
        p.dma(route_s[:, :], route_c, sem="s_y")
        rc_tab = p.dma(tok_s[:, :], tok_c, sem="s_y")
        mz1 = p.op("vector", lambda v: v.memset(ones128r[0:1, :], 1.0))
        p.op("vector", lambda v: v.memset(ones128c[:, :], 1.0))
        mz2 = p.op("vector", lambda v: v.memset(pfill[:, :], 1024))
        mz3 = p.op("vector", lambda v: v.memset(xn_tm[:, 0, :], 0.0))
        mz4 = p.op("vector", lambda v: v.memset(xbuf[:, 0, :], 0.0))
        p.wait("sync", "s_dve", mz4)
        p.dma(xn_scr[1024:1152, :], xn_tm[:, 0, :], sem="s_y")
        xz16 = xbuf[:, 0, :].bitcast(BF16)
        for zi in range(9):
            p.dma(y_scr[zi * 128:(zi + 1) * 128, :], xz16[:, 0:D], sem="s_y")
        zfin = p.dma(elist.rearrange("(p a) o -> p (a o)", p=128), pfill[:, :], sem="s_y")

        wsA = WStream(p, bigA, 4, "scalar")
        for b in range(4):
            wk_ready = wsA.chunk(wk[b * 128:(b + 1) * 128, :],
                                 sA[:, :, b * 128:(b + 1) * 128], D)
        for b in range(4):
            wv_ready = wsA.chunk(wv[b * 128:(b + 1) * 128, :],
                                 sB[:, :, b * 128:(b + 1) * 128], D)
        wgslot = wsA.bf[0]
        p.wait_t("vector", wsA.xp_hist[wsA.i - 2])
        ms = p.op("vector", lambda v: v.memset(wgslot[0:16, :], 0.0), inc="s_dve")
        p.wait("sync", "s_dve", ms)
        p.wait("sync", "s_act", wsA.conv_hist[wsA.i - 1])
        ldg = p.dma(wsA.f32[0][0:8, :], wg, sem="s_w0")
        p.wait_t("scalar", ldg)
        p.wait("scalar", "s_dve", ms)
        cvg = p.op("scalar", lambda s: s.copy(wgslot[0:8, :], wsA.f32[0][0:8, :]),
                   inc="s_act")
        wsA.conv_hist[wsA.i] = cvg
        wsA.i += 1
        p.wait("sync", "s_act", cvg)
        wg_ready = p.xp(wgt[:, :, :], wgslot[0:16, :])
        wsA.xp_hist[wsA.i - 1] = wg_ready

        psV = [bank(0), bank(1)]
        psK = [bank(2), bank(3)]
        psG = [bank(4), bank(5)]

        ld_done = [0] * NTL
        sq_done = [0] * NTL
        var_done = [0] * NTL
        ln_done = [0] * NTL
        xp_done = [0] * NTL
        kv_done = [0] * NTL
        ev_done = [0] * NTL
        kxp_hist = [0] * NTL

        p.wait("sync", "s_y", zfin[1])
        p.wait("scalar", "s_y", zfin[1])
        xn_store = [0] * NTO
        for g in range(2):
            ld_done[g] = p.dma(xbuf[:, g % 2, :], xb[g * 128:(g + 1) * 128, :],
                               sem="s_x%d" % (g % 2))

        def emit_evac(gg):
            own_ = gg < NTO
            p.wait("vector", "s_pe", kv_done[gg])
            if gg >= 4:
                p.wait_t("vector", kxp_hist[gg - 4])
            pr_ = gg % 2
            ps_ = psK[pr_].rearrange("p (h c) -> p h c", c=64)
            cosb_ = cos_s[:, gg, :].unsqueeze(1).broadcast_to([128, 8, 64])
            sinb_ = sin_s[:, gg, :].unsqueeze(1).broadcast_to([128, 8, 64])
            t1_ = scrA[:, :].rearrange("p (h c) -> p h c", c=64)
            t2_ = scrB[:, :].rearrange("p (h c) -> p h c", c=64)
            p.op("vector", lambda v, g=gg, pr=pr_: v.tensor_copy(
                V_aug[:, g, :].rearrange("p (h c) -> p h c", c=65)[:, :, 0:64],
                psV[pr].rearrange("p (h c) -> p h c", c=64)))
            p.op("vector", lambda v, ps=ps_, cosb=cosb_, t1=t1_:
                 v.tensor_tensor(t1, ps, cosb, ALU.mult))
            p.op("vector", lambda v, ps=ps_, sinb=sinb_, t2=t2_:
                 v.tensor_tensor(t2[:, :, 0:32], ps[:, :, 32:64],
                                 sinb[:, :, 0:32], ALU.mult))
            p.op("vector", lambda v, ps=ps_, sinb=sinb_, t2=t2_:
                 v.tensor_tensor(t2[:, :, 32:64], ps[:, :, 0:32],
                                 sinb[:, :, 32:64], ALU.mult))
            ev_done[gg] = p.op("vector", lambda v, g=gg: v.tensor_tensor(
                K_tm[:, g % 4, :], scrA[:, :], scrB[:, :], ALU.add))
            if own_:
                ev_done[gg] = p.op("vector", lambda v, g=gg, pr=pr_: v.tensor_copy(
                    g8[:, g, :], psG[pr][:, 0:8]))
            p.wait("sync", "s_dve", ev_done[gg])
            kxp_hist[gg] = p.xp(K_rt[:, :, gg * 128:(gg + 1) * 128],
                                K_tm[:, gg % 4, :])

        for g in range(NTL):
            sl = g % 2
            own = g < NTO

            p.wait_t("scalar", ld_done[g])
            if g >= 2:
                p.wait_t("scalar", xp_done[g - 2])
                if g - 2 < NTO:
                    p.wait_t("scalar", xn_store[g - 2])
            if g >= 1:
                p.wait("scalar", "s_dve", var_done[g - 1])
            sq_done[g] = p.op("scalar", lambda s, sl=sl: s.activation(
                xn_tm[:, sl, :], xbuf[:, sl, :], AF.Square,
                accum_out=stats[:, 1:2]), inc="s_act")

            p.wait_t("vector", ld_done[g])

            p.op("vector", lambda v, sl=sl: v.reduce_sum(
                stats[:, 0:1], xbuf[:, sl, :], axis=AX))
            p.op("vector", lambda v: v.tensor_scalar(
                stats[:, 2:3], stats[:, 0:1], RECIP_D, None, ALU.mult))
            p.wait("vector", "s_act", sq_done[g])

            p.op("vector", lambda v: v.tensor_scalar(
                stats[:, 3:4], stats[:, 1:2], RECIP_D, None, ALU.mult))
            p.op("vector", lambda v: v.tensor_tensor(
                stats[:, 4:5], stats[:, 2:3], stats[:, 2:3], ALU.mult))
            var_done[g] = p.op("vector", lambda v: v.tensor_tensor(
                stats[:, 3:4], stats[:, 3:4], stats[:, 4:5], ALU.subtract))
            p.wait("scalar", "s_dve", var_done[g])
            sr = p.op("scalar", lambda s: s.activation(
                stats[:, 5:6], stats[:, 3:4], AF.Sqrt, bias=LN_EPS), inc="s_act")
            p.wait("vector", "s_act", sr)

            p.op("vector", lambda v: v.reciprocal(stats[:, 6:7], stats[:, 5:6]))
            p.op("vector", lambda v: v.tensor_scalar(
                stats[:, 7:8], stats[:, 2:3], stats[:, 6:7], -1.0,
                ALU.mult, ALU.mult))
            ln_done[g] = p.op("vector", lambda v, sl=sl: v.tensor_scalar(
                xn_tm[:, sl, :], xbuf[:, sl, :], stats[:, 6:7], stats[:, 7:8],
                ALU.mult, ALU.add))

            p.wait("sync", "s_dve", ln_done[g])
            if own:
                xn_store[g] = p.dma(xn_scr[g * 128:(g + 1) * 128, :],
                                    xn_tm[:, sl, :], sem="s_y")
                dst = xn_own[:, :, g * 128:(g + 1) * 128]
            else:
                if g >= NTO + 3:
                    p.wait("sync", "s_pe", kv_done[g - 3])
                dst = xnf[:, g % 3, :, :]
            xp_done[g] = p.xp(dst, xn_tm[:, sl, :])

            p.wait_t("tensor", xp_done[g])
            p.wait_t("tensor", wv_ready)
            p.wait_t("tensor", wg_ready)
            if g >= 2:
                p.wait("tensor", "s_dve", ev_done[g - 2])

            def kvg_mms(t, g=g, own=own):
                pr = g % 2
                last = None
                for dc in range(NDC):
                    if own:
                        lhsT = xn_own[:, dc, g * 128:(g + 1) * 128]
                    else:
                        lhsT = xnf[:, g % 3, dc, :]
                    st, sp = dc == 0, dc == NDC - 1
                    t.matmul(psV[pr], lhsT, sB[:, dc, :], start=st, stop=sp)
                    last = t.matmul(psK[pr], lhsT, sA[:, dc, :], start=st, stop=sp)
                    if own:
                        last = t.matmul(psG[pr][:, 0:8], lhsT, wgt[:, dc, 0:8],
                                        start=st, stop=sp)
                return last
            kv_done[g] = p.op("tensor", kvg_mms, inc="s_pe")

            if g >= 1:
                emit_evac(g - 1)
            if g + 2 < NTL:
                p.wait("sync", "s_dve", ln_done[g])
                p.wait("sync", "s_act", sq_done[g])
                ld_done[g + 2] = p.dma(xbuf[:, g % 2, :],
                                       xb[(g + 2) * 128:(g + 3) * 128, :],
                                       sem="s_x%d" % (g % 2))

        emit_evac(NTL - 1)

        astack1.close()
        Qp = rstack.enter_context(nc.sbuf_tensor("Qp", [128, NDC, TOWN], BF16, side="right"))
        a1_pe = p.cnt["s_pe"]

        gv = 0
        for tc in range(NTO):
            gg = g8[:, tc, :]
            m1, m2 = stats[:, 0:1], stats[:, 1:2]
            p.op("vector", lambda v, gg=gg, m1=m1: v.reduce_max(m1, gg, axis=AX))
            p.op("vector", lambda v, gg=gg, m1=m1, tc=tc: v.tensor_scalar(
                wgate[:, tc, :], gg, m1, -BIG, ALU.is_ge, ALU.mult))
            p.op("vector", lambda v, gg=gg, tc=tc: v.tensor_tensor(
                wgate[:, tc, :], gg, wgate[:, tc, :], ALU.add))
            p.op("vector", lambda v, m2=m2, tc=tc: v.reduce_max(
                m2, wgate[:, tc, :], axis=AX))
            p.op("vector", lambda v, gg=gg, m2=m2, tc=tc: v.tensor_scalar(
                wgate[:, tc, :], gg, m2, None, ALU.is_ge))
            p.op("vector", lambda v, m1=m1: v.tensor_scalar(
                stats[:, 2:3], m1, -1.0, None, ALU.mult))
            gv = p.op("vector", lambda v, gg=gg: v.tensor_scalar(
                gg, gg, stats[:, 2:3], None, ALU.add))
        p.wait("scalar", "s_dve", gv)

        def gate_exp(s):
            last = None
            for tc in range(NTO):
                last = s.activation(g8[:, tc, :], g8[:, tc, :], AF.Exp)
            return last
        ge = p.op("scalar", gate_exp, inc="s_act")
        p.wait("vector", "s_act", ge)

        for tc in range(NTO):
            p.op("vector", lambda v, tc=tc: v.tensor_tensor(
                g8[:, tc, :], g8[:, tc, :], wgate[:, tc, :], ALU.mult))
            p.op("vector", lambda v, tc=tc: v.reduce_sum(
                stats[:, 3:4], g8[:, tc, :], axis=AX))
            p.op("vector", lambda v: v.reciprocal(stats[:, 4:5], stats[:, 3:4]))
            p.op("vector", lambda v, tc=tc: v.tensor_scalar(
                wgate[:, tc, :], g8[:, tc, :], stats[:, 4:5], None, ALU.mult))

        psR = bank(2)
        psR2 = bank(3)
        for tc in range(NTO):
            p.op("vector", lambda v, tc=tc: v.tensor_scalar(
                selm[:, tc, :], wgate[:, tc, :], 0.0, None, ALU.is_gt))
        cum_ev = [0] * NTO
        for tc in range(NTO):
            sv = p.cnt["s_dve"]
            p.wait("tensor", "s_dve", sv)
            if tc >= 1:
                p.wait("tensor", "s_dve", cum_ev[tc - 1])
            mmv = p.op("tensor", lambda t, tc=tc: t.matmul(
                psR[:, 0:8], triu_s[:, :], selm[:, tc, :],
                start=True, stop=True), inc="s_pe")
            mmt = p.op("tensor", lambda t, tc=tc: t.matmul(
                psR2[0:1, 0:8], ones128c[:, :], selm[:, tc, :],
                start=True, stop=True), inc="s_pe")
            p.wait("vector", "s_pe", mmt)
            p.op("vector", lambda v, tc=tc: v.tensor_copy(
                ridx[:, tc, :], psR[:, 0:8]))
            cum_ev[tc] = p.op("vector", lambda v, tc=tc: v.tensor_copy(
                rtot[0:1, tc * 8:tc * 8 + 8], psR2[0:1, 0:8]))
        p.op("vector", lambda v: v.memset(rtot[0:1, 64:72], 0.0))
        for tc in range(1, NTO):
            p.op("vector", lambda v, tc=tc: v.tensor_tensor(
                rtot[0:1, 72:80], rtot[0:1, 64:72],
                rtot[0:1, (tc - 1) * 8:tc * 8], ALU.add))
            p.op("vector", lambda v: v.tensor_copy(rtot[0:1, 64:72],
                                                   rtot[0:1, 72:80]))
            ro = p.cnt["s_dve"]
            p.wait("tensor", "s_dve", ro)
            rr = p.op("tensor", lambda t, tc=tc: t.matmul(
                psR2[:, 8:16], ones128r[0:1, :], rtot[0:1, 64:72],
                start=True, stop=True), inc="s_pe")
            p.wait("vector", "s_pe", rr)
            p.op("vector", lambda v, tc=tc: v.tensor_tensor(
                ridx[:, tc, :], ridx[:, tc, :], psR2[:, 8:16], ALU.add))
        for tc in range(NTO):
            p.op("vector", lambda v, tc=tc: v.tensor_scalar(
                ridx[:, tc, :], ridx[:, tc, :], -1.0, None, ALU.add))
            p.op("vector", lambda v, tc=tc: v.tensor_tensor(
                ridx[:, tc, :], ridx[:, tc, :], selm[:, tc, :], ALU.mult))
            p.op("vector", lambda v, tc=tc: v.tensor_scalar(
                rtot[:, 0:8], selm[:, tc, :], -1.0, 1.0, ALU.mult, ALU.add))
            p.op("vector", lambda v, tc=tc: v.tensor_scalar(
                rtot[:, 0:8], rtot[:, 0:8], route_s[:, 8:9], None, ALU.mult))
            p.op("vector", lambda v, tc=tc: v.tensor_tensor(
                ridx[:, tc, :], ridx[:, tc, :], rtot[:, 0:8], ALU.add))
            p.op("vector", lambda v, tc=tc: v.tensor_tensor(
                ridx[:, tc, :], ridx[:, tc, :], route_s[:, 0:8], ALU.add))
            p.op("vector", lambda v, tc=tc: v.tensor_copy(
                ridxu[:, tc, :], ridx[:, tc, :]))
        rfin = p.cnt["s_dve"]
        p.wait("sync", "s_dve", rfin)
        p.dma(w_scr[0:TOWN, :].rearrange("(a p) e -> p a e", p=128),
              wgate[:, :, :], sem="s_y")
        p.op("vector", lambda v: v.memset(rtot[:, 0:8], 0.0))
        p.wait("sync", "s_dve", p.cnt["s_dve"])
        p.dma(w_scr[1024:1152, :], rtot[:, 0:8], sem="s_y")
        p.wait("sync", "s_y", zfin[1])
        p.wait("gpsimd", "s_y", zfin[1])
        p.wait("gpsimd", "s_dve", rfin)
        for e in range(NE):
            for tc in range(NTO):
                rt = p.gdma(lambda g, e=e, tc=tc:
                            g.indirect_dma_start(
                                elist,
                                bass.IndirectOffsetOnAxis(
                                    ap=ridxu[:, tc, e:e + 1], axis=0),
                                tok_s[:, 2 * tc:2 * tc + 2], None))
        route_done = [rt]

        psQ = [bank(0), bank(1)]
        q_ev = [0] * (4 * NTO)
        q_xp = [0] * (4 * NTO)
        qmm_by_slot = [a1_pe, a1_pe]
        for es in range(4):
            slot = sA if es % 2 == 0 else sB
            last_xp = 0
            for b in range(4):
                c = es * 4 + b
                htop = (c // 4) * 8 + (c % 4)
                last_xp = wsA.chunk(
                    wq[htop * 64:(htop + 1) * 64, :],
                    slot[:, :, b * 128:(b + 1) * 128], D,
                    war_val=qmm_by_slot[es % 2],
                    src2=wq[(htop + 4) * 64:(htop + 5) * 64, :])
            p.wait_t("tensor", last_xp)
            last_mm = 0
            for tc in range(NTO):
                fl = es * NTO + tc
                pr = fl % 2
                if fl >= 2:
                    p.wait("tensor", "s_dve", q_ev[fl - 2])
                else:
                    p.wait("tensor", "s_dve", ev_done[NTL - 1])

                def q_mms(t, tc=tc, slot=slot, pr=pr):
                    last = None
                    for dc in range(NDC):
                        last = t.matmul(psQ[pr],
                                        xn_own[:, dc, tc * 128:(tc + 1) * 128],
                                        slot[:, dc, :],
                                        start=(dc == 0), stop=(dc == NDC - 1))
                    return last
                last_mm = p.op("tensor", q_mms, inc="s_pe")

                p.wait("vector", "s_pe", last_mm)
                if fl >= 2:
                    p.wait_t("vector", q_xp[fl - 2])

                ps_ = psQ[pr].rearrange("p (h c) -> p h c", c=64)
                cosb_ = cos_s[:, tc, :].unsqueeze(1).broadcast_to([128, 8, 64])
                sinb_ = sin_s[:, tc, :].unsqueeze(1).broadcast_to([128, 8, 64])
                t1_ = scrA[:, :].rearrange("p (h c) -> p h c", c=64)
                t2_ = scrB[:, :].rearrange("p (h c) -> p h c", c=64)
                p.op("vector", lambda v, ps=ps_, cosb=cosb_, t1=t1_:
                     v.tensor_tensor(t1, ps, cosb, ALU.mult))
                p.op("vector", lambda v, ps=ps_, sinb=sinb_, t2=t2_:
                     v.tensor_tensor(t2[:, :, 0:32], ps[:, :, 32:64],
                                     sinb[:, :, 0:32], ALU.mult))
                p.op("vector", lambda v, ps=ps_, sinb=sinb_, t2=t2_:
                     v.tensor_tensor(t2[:, :, 32:64], ps[:, :, 0:32],
                                     sinb[:, :, 32:64], ALU.mult))
                q_ev[fl] = p.op("vector", lambda v, fl=fl: v.tensor_tensor(
                    Q_tm[:, fl % 2, :], scrA[:, :], scrB[:, :], ALU.add))
                p.wait("sync", "s_dve", q_ev[fl])
                q_xp[fl] = p.xp(
                    Qp[:, es * 4:(es + 1) * 4, tc * 128:(tc + 1) * 128],
                    Q_tm[:, fl % 2, :])
            qmm_by_slot[es % 2] = last_mm
        qmm_done = max(qmm_by_slot)

        p.barrier(exclude=("s_ca", "s_cb", "s_cc"))
        astack2.close()
        bstack = contextlib.ExitStack()
        attn = bstack.enter_context(nc.sbuf_tensor("attn", [128, NDC, TOWN], BF16))
        Ebuf = bstack.enter_context(nc.sbuf_tensor("Ebuf", [128, 3, 1024], BF16))
        rrow = bstack.enter_context(nc.sbuf_tensor("rrow", [128, 512], F32))
        nrm = bstack.enter_context(nc.sbuf_tensor("nrm", [128, 512], BF16))
        ones64 = bstack.enter_context(nc.sbuf_tensor("ones64", [1, 64], F32))
        ov = p.op("vector", lambda v: v.memset(ones64[0:1, :], 1.0), inc="s_dve")
        p.wait("tensor", "s_dve", ov)

        psS = [pbig[:, 0:1024], pbig[:, 1024:2048], pbig[:, 2048:3072]]
        psPV = [pbig[:, 3072:3584], pbig[:, 3584:4096]]

        GROUPS = [(2 * i, 2 * i + 2) for i in range(16)]
        exp_hist = []
        pv_hist = []
        mult_hist = []
        sBflat = sB[:, :, :].rearrange("p a b -> p (a b)")
        wo_pref = {}
        wo_war = {}

        def prefetch_wo_ec0():
            last = 0
            prev_cv = 0
            prev_xp = 0
            for b in range(4):
                f32v = sBflat[:, 4096:8192].bitcast(F32)
                if prev_cv:
                    p.wait("sync", "s_dve", prev_cv)
                ld = p.dma(f32v, wo[b * 128:(b + 1) * 128, :], sem="s_w0")
                p.wait_t("vector", ld)
                if prev_xp:
                    p.wait_t("vector", prev_xp)
                cv = p.op("vector", lambda v, o=sBflat[:, 0:2048], i_=f32v:
                          v.tensor_copy(o, i_), inc="s_dve")
                prev_cv = cv
                p.wait("sync", "s_dve", cv)
                p.wait("sync", "s_pe", qmm_by_slot[0])
                last = p.xp(sA[:, :, b * 128:(b + 1) * 128], sBflat[:, 0:2048])
                prev_xp = last
            wo_pref[0] = last

        prefetch_wo_ec0()

        rb_borrow = {0: 0, 1: 0, 2: 0}
        gabs_ctr = [0]

        def emit_s(gabs, c, st2_, lc):
            kc = c // 4
            qoff = st2_ * 512
            sl = gabs % 3
            if gabs >= 3:
                p.wait("tensor", "s_act", exp_hist[gabs - 3])
            p.wait("tensor", "s_dve", rb_borrow[sl])

            def s_mms(t, sl=sl, kc=kc, c=c, qoff=qoff, lc=lc):
                t.matmul(psS[sl][:, 0:512],
                         K_rt[0:64, kc, lc * 128:(lc + 1) * 128],
                         Qp[0:64, c, qoff:qoff + 512],
                         start=True, stop=True)
                return t.matmul(psS[sl][:, 512:1024],
                                K_rt[64:128, kc, lc * 128:(lc + 1) * 128],
                                Qp[64:128, c, qoff:qoff + 512],
                                start=True, stop=True)
            sv = p.op("tensor", s_mms, inc="s_pe")
            p.wait("scalar", "s_pe", sv)
            if gabs >= 3:
                p.wait("scalar", "s_pe", pv_hist[gabs - 3])
            ev = p.op("scalar", lambda s, sl=sl: s.activation(
                Ebuf[:, sl, :], psS[sl][:, :], AF.Exp,
                bias=EXP_BIAS, scale=0.125), inc="s_act")
            exp_hist.append(ev)

        def emit_pv(gabs, c, lc):
            kv_e = (c // 4) * 2
            sl = gabs % 3
            p.wait("tensor", "s_act", exp_hist[gabs])
            if lc == 0 and len(mult_hist) >= 2:
                p.wait("tensor", "s_dve", mult_hist[-1])

            def pv_mms(t, sl=sl, kv_e=kv_e, lc=lc):
                t.matmul(psPV[0][0:65, :],
                         V_aug[:, lc, kv_e * 65:kv_e * 65 + 65],
                         Ebuf[:, sl, 0:512],
                         start=(lc == 0), stop=(lc == NTL - 1))
                return t.matmul(psPV[1][0:65, :],
                                V_aug[:, lc, (kv_e + 1) * 65:(kv_e + 1) * 65 + 65],
                                Ebuf[:, sl, 512:1024],
                                start=(lc == 0), stop=(lc == NTL - 1))
            pv_hist.append(p.op("tensor", pv_mms, inc="s_pe"))

        for c in range(16):
            he = (c // 4) * 8 + (c % 4)
            for st2 in range(2):
                qoff = st2 * 512
                g0 = gabs_ctr[0]
                for lc in range(NTL):
                    emit_s(g0 + lc, c, st2, lc)
                    if lc >= 2:
                        emit_pv(g0 + lc - 2, c, lc - 2)
                emit_pv(g0 + NTL - 2, c, NTL - 2)
                emit_pv(g0 + NTL - 1, c, NTL - 1)
                gabs_ctr[0] = g0 + NTL

                sl_last = (g0 + NTL - 1) % 3
                p.wait("vector", "s_pe", pv_hist[-1])
                for j, h in ((0, he), (1, he + 4)):
                    ec, qrow = h // 2, 64 * (h % 2)
                    rbt = psS[sl_last][:, j * 512:(j + 1) * 512]
                    rc = p.op("vector", lambda v, j=j: v.reciprocal(
                        rrow[0:1, :], psPV[j][64:65, :]))
                    p.wait("tensor", "s_dve", rc)
                    rb = p.op("tensor", lambda t, rbt=rbt: t.matmul(
                        rbt[0:64, :], ones64[0:1, :], rrow[0:1, :],
                        start=True, stop=True), inc="s_pe")
                    p.wait("vector", "s_pe", rb)
                    rbc = p.op("vector", lambda v, rbt=rbt: v.tensor_copy(
                        rrow[0:64, :], rbt[0:64, :]))
                    rb_borrow[sl_last] = rbc
                    if qrow == 0:
                        mh = p.op("vector", lambda v, ec=ec, qoff=qoff, j=j:
                                  v.tensor_tensor(attn[0:64, ec, qoff:qoff + 512],
                                                  psPV[j][0:64, :], rrow[0:64, :],
                                                  ALU.mult))
                    else:
                        p.op("vector", lambda v, j=j: v.tensor_tensor(
                            nrm[0:64, :], psPV[j][0:64, :], rrow[0:64, :],
                            ALU.mult))
                        mh = p.op("vector", lambda v, ec=ec, qoff=qoff:
                                  v.tensor_copy(attn[64:128, ec, qoff:qoff + 512],
                                                nrm[0:64, :]))
                    mult_hist.append(mh)

        p.barrier(exclude=("s_ca", "s_cb", "s_cc"))
        rstack.close()
        cstack = contextlib.ExitStack()
        yb = cstack.enter_context(nc.sbuf_tensor("yb", [128, NTO, D], F32, side="right"))
        c2stack = contextlib.ExitStack()
        cstage = c2stack.enter_context(nc.sbuf_tensor("cstage", [128, 8, D], BF16, side="right"))
        wsC = WStream(p, cstage, 8, "scalar")
        psY = [bank(0), bank(1)]

        y_init = p.dma(yb[:, :, :],
                       xb[0:TOWN, :].rearrange("(tc p) d -> p tc d", p=128),
                       sem="s_y")

        c_ev = [0, 0]
        last_c_mm = 0
        wo_war = {0: qmm_by_slot[1], 1: qmm_by_slot[1]}
        for ec in range(4):
            slot = sA if ec % 2 == 0 else sB
            if ec >= 1:
                for b in range(4):
                    eb = ec * 4 + b
                    wo_pref[ec] = wsC.chunk(wo[eb * 128:(eb + 1) * 128, :],
                                            slot[:, :, b * 128:(b + 1) * 128], D,
                                            war_val=wo_war[ec % 2])
            p.wait_t("tensor", wo_pref[ec])
            for tc in range(NTO):
                pr = tc % 2
                p.wait("tensor", "s_dve", c_ev[pr])

                def wo_mms(t, tc=tc, slot=slot, pr=pr):
                    last = None
                    for dc in range(NDC):
                        last = t.matmul(psY[pr],
                                        attn[:, dc, tc * 128:(tc + 1) * 128],
                                        slot[:, dc, :],
                                        start=(dc == 0), stop=(dc == NDC - 1))
                    return last
                mv = p.op("tensor", wo_mms, inc="s_pe")
                last_c_mm = mv
                p.wait("vector", "s_pe", mv)
                p.wait_t("vector", y_init)
                c_ev[pr] = p.op("vector", lambda v, tc=tc, ec=ec, pr=pr:
                                v.tensor_tensor(
                                    yb[:, tc, ec * 512:(ec + 1) * 512], psY[pr],
                                    yb[:, tc, ec * 512:(ec + 1) * 512], ALU.add),
                                inc="s_dve")
            wo_war[ec % 2] = last_c_mm

        p.barrier(exclude=("s_ca", "s_cb", "s_cc"))
        bstack.close()
        c2stack.close()
        dstack = contextlib.ExitStack()
        hprime = dstack.enter_context(nc.sbuf_tensor("hprime", [128, NHC, CAP], BF16))
        aprime = dstack.enter_context(nc.sbuf_tensor("aprime", [128, 512], BF16))
        bigD = dstack.enter_context(nc.sbuf_tensor("bigD", [128, 13824], BF16))
        xe_tm = dstack.enter_context(nc.sbuf_tensor("xe_tm", [128, D], BF16))
        xe_fm = dstack.enter_context(nc.sbuf_tensor("xe_fm", [128, NDC, CAP], BF16))
        ybuf = dstack.enter_context(nc.sbuf_tensor("ybuf", [128, 3, D], BF16))
        eids = dstack.enter_context(nc.sbuf_tensor("eids", [128, 2, 8], mybir.dt.int32))
        wloc = dstack.enter_context(nc.sbuf_tensor("wloc", [128, 2, 24], F32))
        wcol = dstack.enter_context(nc.sbuf_tensor("wcol", [128, 2, 4], F32))
        wstg2 = dstack.enter_context(nc.sbuf_tensor("wstg2", [128, D], F32))
        wsD = WStream(p, bigD[:, 0:8192].rearrange("p (a b) -> p a b", b=D),
                      4, "vector", extra_f32=[wstg2[:, :]], nsem=4)

        wsl2 = xn_own[:, :, 0:512]
        w3r2 = xn_own[:, 0:NHC, 512:1024]
        psAB = [bank(0), bank(1), bank(2), bank(3)]
        w1_slots = [sA[:, :, 0:128], sA[:, :, 128:256],
                    wsl2[:, :, 0:128], wsl2[:, :, 128:256]]
        w2_slots = [sA[:, :, 256:384], sA[:, :, 384:512],
                    wsl2[:, :, 256:384], wsl2[:, :, 384:512]]
        w3_ring1 = bigD[:, 8192:8192 + NHC * 512] \
            .rearrange("p (h c) -> p h c", c=512)
        w3_slots = [sB[:, 0:NHC, 0:512], w3_ring1, w3r2]

        hb_ctr = [0]
        w3_ctr = [0]
        w12_war = [last_c_mm] * 4
        w3_war = [last_c_mm] * 3
        hp_done = [0] * NHC
        ab_free = [0, 0, 0, 0]
        yv_done = [0, 0, 0, 0]
        hp_war = [0]
        w12_xp = {}
        w3_xp = {}
        gath_xp = {}
        gath_w = {}
        scat_done = {0: None}

        def emit_w12_chunks(e, hb):
            r = hb_ctr[0] % 4
            hb_ctr[0] += 1
            wsD.chunk2(w1_slots[r], war_val=w12_war[r])
            w12_xp[(e, hb)] = wsD.chunk2(w2_slots[r], war_val=w12_war[r])
            return r

        def emit_w3_chunks(e, dcol):
            r = w3_ctr[0] % 3
            w3_ctr[0] += 1
            for b in range(4):
                w3_xp[(e, dcol)] = wsD.chunk2(
                    w3_slots[r][:, :, b * 128:(b + 1) * 128],
                    war_val=w3_war[r])
            return r

        def prefetch_tokens(e, war_pe):
            pe = e % 2
            p.wait_t("sync", scat_done.get(e))
            for rt in route_done:
                p.wait_t("sync", rt)
            el = p.dma(eids[:, pe, 0:6],
                       elist[e * 640:e * 640 + CAP, :].rearrange(
                           "(p a) o -> p (a o)", a=3), sem="s_y")
            p.wait_t("gpsimd", el)
            for rt in route_done:
                p.wait_t("gpsimd", rt)
            for st in range(3):
                off = eids[:, pe, 2 * st:2 * st + 1].bitcast(mybir.dt.uint32)
                gx = p.gdma(lambda g, off=off: g.indirect_dma_start(
                    xe_tm[:, :], None, xn_scr,
                    bass.IndirectOffsetOnAxis(ap=off, axis=0)))
                p.wait_t("sync", gx)
                p.wait("sync", "s_pe", war_pe)
                gath_xp[e] = p.xp(xe_fm[:, :, st * 128:(st + 1) * 128],
                                  xe_tm[:, :])
                p.wait_t("gpsimd", gath_xp[e])
                gw = p.gdma(lambda g, off=off, pe=pe, st=st:
                            g.indirect_dma_start(
                                wloc[:, pe, st * 8:(st + 1) * 8], None, w_scr,
                                bass.IndirectOffsetOnAxis(ap=off, axis=0)),
                            sem="s_cc")
            p.wait_t("vector", gw)
            gath_w[e] = p.op("vector", lambda v, pe=pe, e=e: v.tensor_copy(
                wcol[:, pe, 0:3],
                wloc[:, pe, :].rearrange("p (s e) -> p s e", e=8)[:, :, e]))

        def emit_ab(e, hb, r):
            p.wait_t("tensor", w12_xp[(e, hb)])
            p.wait_t("tensor", gath_xp[e])
            for i in range(2):
                p.wait("tensor", "s_dve", ab_free[i])
                p.wait("tensor", "s_dve", yv_done[i])

            def ab_mms(t, r=r):
                last = None
                for dc in range(NDC):
                    st, sp = dc == 0, dc == NDC - 1
                    t.matmul(psAB[0][:, 0:CAP], w1_slots[r][:, dc, :],
                             xe_fm[:, dc, :], start=st, stop=sp)
                    last = t.matmul(psAB[1][:, 0:CAP], w2_slots[r][:, dc, :],
                                    xe_fm[:, dc, :], start=st, stop=sp)
                return last
            last_ab = p.op("tensor", ab_mms, inc="s_pe")
            p.wait("scalar", "s_pe", last_ab)
            p.wait("scalar", "s_dve", ab_free[0])
            av = p.op("scalar", lambda s: s.activation(
                aprime[:, 0:CAP], psAB[0][:, 0:CAP], AF.Silu), inc="s_act")
            p.wait("vector", "s_act", av)
            p.wait("vector", "s_pe", hp_war[0])

            def h_mul(v, hb=hb):
                return v.tensor_tensor(hprime[:, hb, :], psAB[1][:, 0:CAP],
                                       aprime[:, 0:CAP], ALU.mult)
            hv = p.op("vector", h_mul, inc="s_dve")
            ab_free[0] = hv
            ab_free[1] = hv
            hp_done[hb] = hv
            w12_war[r] = last_ab
            return last_ab

        def emit_y(e, dcol, r):
            pe = e % 2
            p.wait_t("tensor", w3_xp[(e, dcol)])
            for hb in range(NHC):
                p.wait("tensor", "s_dve", hp_done[hb])
            last_y = 0
            for st in range(3):
                pr = (dcol * 3 + st) % 4
                p.wait("tensor", "s_dve", yv_done[pr])

                def y_mms(t, st=st, r=r, pr=pr):
                    last = None
                    for hc in range(NHC):
                        last = t.matmul(psAB[pr],
                                        hprime[:, hc, st * 128:(st + 1) * 128],
                                        w3_slots[r][:, hc, :],
                                        start=(hc == 0), stop=(hc == NHC - 1))
                    return last
                yv = p.op("tensor", y_mms, inc="s_pe")
                last_y = yv
                p.wait("vector", "s_pe", yv)
                p.wait("vector", "s_dve", gath_w[e])
                if scat_done.get(e) is not None:
                    p.wait_t("vector", scat_done[e])

                def y_ev(v, st=st, dcol=dcol, pe=pe, pr=pr):
                    return v.tensor_scalar(
                        ybuf[:, st, dcol * 512:(dcol + 1) * 512],
                        psAB[pr], wcol[:, pe, st:st + 1], None, ALU.mult)
                yv_done[pr] = p.op("vector", y_ev, inc="s_dve")
            w3_war[r] = last_y
            return last_y

        wsD.srcs = []
        for e in range(NE):
            for hb in range(NHC):
                wsD.srcs.append((w1[e, hb * 128:(hb + 1) * 128, :], D, None))
                wsD.srcs.append((w2[e, hb * 128:(hb + 1) * 128, :], D, None))
            for dc in range(4):
                for b in range(4):
                    db = dc * 4 + b
                    wsD.srcs.append((w3[e, db * 128:(db + 1) * 128, :], HF,
                                     None))

        issue_list = []
        pos12, pos3 = [], []
        for e in range(NE):
            for hb in range(NHC):
                pos12.append(len(issue_list))
                issue_list.append(("w12", e, hb, len(pos12) - 1))
            for dc in range(4):
                pos3.append(len(issue_list))
                issue_list.append(("w3", e, dc, len(pos3) - 1))
        ip = [0]

        def ensure_issued(upto, k_emitted):
            while ip[0] <= min(upto, len(issue_list) - 1):
                kind, e_, i_, g = issue_list[ip[0]]
                if kind == "w12":
                    pred = pos12[g - 4] if g >= 4 else -1
                else:
                    pred = pos3[g - 3] if g >= 3 else -1
                if pred >= k_emitted:
                    break
                if kind == "w12":
                    emit_w12_chunks(e_, i_)
                else:
                    emit_w3_chunks(e_, i_)
                ip[0] += 1

        LA = 3
        for e in range(NE):
            if e == 0:
                ensure_issued(LA, 0)
                prefetch_tokens(0, last_c_mm)
            for hb in range(NHC):
                k = e * 15 + hb
                ensure_issued(k + LA, k)
                r = (e * NHC + hb) % 4
                la = emit_ab(e, hb, r)
            scat_done[e + 1] = scat_done.get(e)
            for dcol in range(4):
                k = e * 15 + 11 + dcol
                ensure_issued(k + LA, k)
                r = (e * 4 + dcol) % 3
                if dcol == 0 and e + 1 < NE:
                    prefetch_tokens(e + 1, la)
                ly = emit_y(e, dcol, r)
            hp_war[0] = ly
            pe = e % 2
            sd = None
            for st in range(3):
                p.wait("gpsimd", "s_dve", yv_done[(3 * 3 + st) % 4])
                p.wait("gpsimd", "s_dve", p.cnt["s_dve"])
                off = eids[:, pe, 2 * st:2 * st + 1].bitcast(mybir.dt.uint32)
                sd = p.gdma(lambda g, off=off, st=st, pe=pe:
                            g.indirect_dma_start(
                                y_scr,
                                bass.IndirectOffsetOnAxis(ap=off, axis=0),
                                ybuf[:, st, :], None,
                                compute_op=ALU.add), sem="s_cb")
            scat_done[e + 1] = sd

        p.barrier(("sync", "vector"))
        yv2 = bigD[:, 0:4096].rearrange("p (a b) -> p a b", b=D)
        for q in range(4):
            ldh = p.dma(yv2[:, 0:2, :],
                        y_scr[q * 256:(q + 1) * 256, :].rearrange(
                            "(a p) d -> p a d", p=128), sem="s_y")
            p.wait_t("vector", ldh)
            for a in range(2):
                p.op("vector", lambda v, a=a, q=q: v.tensor_tensor(
                    yb[:, q * 2 + a, :], yb[:, q * 2 + a, :], yv2[:, a, :],
                    ALU.add))
            bw = p.cnt["s_dve"]
            p.wait("sync", "s_dve", bw)

        p.barrier(("sync",))
        stv = p.dma(out.rearrange("(tc p) d -> p tc d", p=128), yb[:, :, :],
                    sem="s_y")
        p.wait_t("sync", stv)

        dstack.close()
        cstack.close()

        with nc.Block() as block:
            @block.sync
            def _(e):
                for fn in p.q["sync"]:
                    fn(e)

            @block.vector
            def _(e):
                for fn in p.q["vector"]:
                    fn(e)

            @block.scalar
            def _(e):
                for fn in p.q["scalar"]:
                    fn(e)

            @block.gpsimd
            def _(e):
                for fn in p.q["gpsimd"]:
                    fn(e)

            @block.tensor
            def _(e):
                for fn in p.q["tensor"]:
                    fn(e)

    return nc


def _rope_tables():
    half = 32
    inv = 1.0 / (10000.0 ** (np.arange(half, dtype=np.float64) / half))
    pos = np.arange(LBATCH, dtype=np.float64)
    ang = pos[:, None] * inv[None, :]
    cos64 = np.concatenate([np.cos(ang), np.cos(ang)], axis=1).astype(np.float32)
    sin64 = np.concatenate([-np.sin(ang), np.sin(ang)], axis=1).astype(np.float32)
    return cos64, sin64


def _route_consts():
    triu = np.triu(np.ones((128, 128), dtype=np.float32)).astype(bfdt)
    route = np.zeros((128, 16), dtype=np.float32)
    for e in range(8):
        route[:, e] = e * 640.0
    route[:, 8] = 512.0 + np.arange(128)
    tok = np.zeros((128, 16), dtype=np.int32)
    for tc in range(8):
        tok[:, 2 * tc] = tc * 128 + np.arange(128)
        tok[:, 2 * tc + 1] = tc * 128 + np.arange(128)
    return triu, route, tok


def kernel(L0_attn_x, L0_attn_Wq, L0_attn_Wk, L0_attn_Wv, L0_attn_Wo,
           L0_ffn_Wg, L0_ffn_W1, L0_ffn_W2, L0_ffn_W3):
    x = np.ascontiguousarray(np.asarray(L0_attn_x, dtype=np.float32))
    cos64, sin64 = _rope_tables()
    triu, route, tok = _route_consts()
    nc = build_nc()
    in_maps = []
    for c in range(8):
        b, qs = c // 4, (c % 4) * TOWN
        xr = np.ascontiguousarray(np.roll(x[b], -qs, axis=0))
        ct = np.ascontiguousarray(
            np.roll(cos64, -qs, axis=0).reshape(NTL, 128, HD).transpose(1, 0, 2)
        ).astype(bfdt)
        st = np.ascontiguousarray(
            np.roll(sin64, -qs, axis=0).reshape(NTL, 128, HD).transpose(1, 0, 2)
        ).astype(bfdt)
        in_maps.append({
            "xb": xr,
            "wq": np.ascontiguousarray(L0_attn_Wq, dtype=np.float32),
            "wk": np.ascontiguousarray(L0_attn_Wk, dtype=np.float32),
            "wv": np.ascontiguousarray(L0_attn_Wv, dtype=np.float32),
            "wo": np.ascontiguousarray(L0_attn_Wo, dtype=np.float32),
            "wg": np.ascontiguousarray(L0_ffn_Wg, dtype=np.float32),
            "w1": np.ascontiguousarray(L0_ffn_W1, dtype=np.float32),
            "w2": np.ascontiguousarray(L0_ffn_W2, dtype=np.float32),
            "w3": np.ascontiguousarray(L0_ffn_W3, dtype=np.float32),
            "cos_t": ct,
            "sin_t": st,
            "triu_c": triu,
            "route_c": route,
            "tok_c": tok,
        })
    res = run_bass_kernel_spmd(nc, in_maps, core_ids=list(range(8)))
    outf = np.empty((2, LBATCH, D), dtype=np.float32)
    for c in range(8):
        b, qs = c // 4, (c % 4) * TOWN
        outf[b, qs:qs + TOWN] = res.results[c]["out"]
    return outf


# version 14 (final)
# speedup vs baseline: 1.0189x; 1.0189x over previous
import contextlib

import numpy as np
import ml_dtypes
import concourse.bass as bass
import concourse.mybir as mybir
from concourse.bass_utils import run_bass_kernel_spmd

F32 = mybir.dt.float32
BF16 = mybir.dt.bfloat16
AF = mybir.ActivationFunctionType
ALU = mybir.AluOpType
AX = mybir.AxisListType.X
bfdt = ml_dtypes.bfloat16

D = 2048
LBATCH = 4096
TOWN = 1024
NH = 32
NKV = 8
HD = 64
NE = 8
HF = 1408
NDC = 16
NHC = 11
NTL = 32
NTO = 8

EXP_BIAS = -16.0
LN_EPS = 1e-5
RECIP_D = 1.0 / 2048.0
BIG = 1e30

ENGS = ("sync", "gpsimd", "vector", "scalar", "tensor")


DMA_SEMS = ("s_xp", "s_x0", "s_x1", "s_w0", "s_w1", "s_w2", "s_w3",
            "s_y", "s_ca", "s_cb", "s_cc")


class Pl:

    def __init__(self, sems):
        self.q = {e: [] for e in ENGS}
        self.sems = sems
        self.cnt = {k: 0 for k in sems}
        self.last_wait = {e: {k: 0 for k in sems} for e in ENGS}
        self.xp_pending = 0

    def wait(self, eng, sem, val):
        if val is None or val <= 0 or self.last_wait[eng][sem] >= val:
            return
        self.last_wait[eng][sem] = val
        h = self.sems[sem]
        self.q[eng].append(lambda e, h=h, v=val: e.wait_ge(h, v))

    def wait_t(self, eng, tok):
        if tok:
            self.wait(eng, tok[0], tok[1])

    def op(self, eng, fn, inc=None, n=1):
        if eng == "vector":
            self.wait(eng, "s_dve", self.cnt["s_dve"])
            inc, n = "s_dve", 1
        elif eng == "scalar":
            self.wait(eng, "s_act", self.cnt["s_act"])
            inc, n = "s_act", 1
        if inc is None:
            self.q[eng].append(fn)
            return None
        h = self.sems[inc]
        self.cnt[inc] += n
        self.q[eng].append(lambda e, fn=fn, h=h, n=n: fn(e).then_inc(h, n))
        return self.cnt[inc]

    def dma(self, out_ap, in_ap, sem="s_y"):
        if sem == "s_y":
            self.wait("sync", "s_y", self.cnt["s_y"])
        v = self.op("sync", lambda e, o=out_ap, i=in_ap: e.dma_start(o, i),
                    inc=sem, n=16)
        return (sem, v)

    def xp(self, out_ap, in_ap):
        self.wait("sync", "s_xp", self.cnt["s_xp"])
        v = self.op("sync", lambda e, o=out_ap, i=in_ap:
                    e.dma_start(o, i, transpose=True), inc="s_xp", n=16)
        self.xp_pending = v
        return ("s_xp", v)

    def gdma(self, fn, sem="s_ca"):
        self.wait("gpsimd", sem, self.cnt[sem])
        return (sem, self.op("gpsimd", fn, inc=sem, n=16))

    def barrier(self, engines=ENGS, exclude=()):
        tot = dict(self.cnt)
        for e in engines:
            for s, v in tot.items():
                if s in exclude:
                    continue
                self.wait(e, s, v)


class WStream:

    def __init__(self, p, buf, nslots, conv, extra_f32=(), nsem=2):
        self.p = p
        n = nslots
        flat = buf[:, :, :].rearrange("p a b -> p (a b)")
        half = n * 2048 // 2
        self.bf = [flat[:, i * 2048:(i + 1) * 2048] for i in range(n // 2)]
        f32v = flat[:, half:2 * half].bitcast(F32)
        self.f32 = [f32v[:, i * 2048:(i + 1) * 2048] for i in range(n // 4)] \
            if n >= 4 else [f32v[:, 0:1024]]
        self.f32 += list(extra_f32)
        self.nsem = nsem
        self.conv = conv
        self.csem = "s_act" if conv == "scalar" else "s_dve"
        self.i = 0
        self.conv_hist = {}
        self.xp_hist = {}
        self.srcs = None
        self.ld_hist = {}

    def _load(self, j):
        src, cols, src2 = self.srcs[j]
        fs = self.f32[j % len(self.f32)]
        dsem = "s_w%d" % (j % self.nsem)
        if src2 is None:
            ld = self.p.dma(fs[:, 0:cols], src, sem=dsem)
        else:
            self.p.dma(fs[0:64, 0:cols], src, sem=dsem)
            ld = self.p.dma(fs[64:128, 0:cols], src2, sem=dsem)
        self.ld_hist[j] = ld

    def chunk2(self, dst_ap, war_val=0):
        p = self.p
        i = self.i
        self.i += 1
        nf, nb = len(self.f32), len(self.bf)
        if i not in self.ld_hist:
            for j in range(i, min(i + nf, len(self.srcs))):
                if j - nf >= 0 and (j - nf) in self.conv_hist:
                    p.wait("sync", self.csem, self.conv_hist[j - nf])
                self._load(j)
        cols = self.srcs[i][1]
        fs, bs = self.f32[i % nf], self.bf[i % nb]
        p.wait_t(self.conv, self.ld_hist[i])
        if i >= nb:
            p.wait_t(self.conv, self.xp_hist[i - nb])
        if self.conv == "scalar":
            cv = p.op("scalar",
                      lambda s_, o=bs[:, 0:cols], i_=fs[:, 0:cols]:
                      s_.copy(o, i_), inc=self.csem)
        else:
            cv = p.op("vector",
                      lambda v, o=bs[:, 0:cols], i_=fs[:, 0:cols]:
                      v.tensor_copy(o, i_), inc=self.csem)
        self.conv_hist[i] = cv
        p.wait("sync", self.csem, cv)
        if i + nf < len(self.srcs):
            self._load(i + nf)
        p.wait("sync", "s_pe", war_val)
        xv = p.xp(dst_ap, bs[:, 0:cols])
        self.xp_hist[i] = xv
        return xv

    def chunk(self, src_ap, dst_ap, cols, war_val=0, src2=None):
        p = self.p
        i = self.i
        self.i += 1
        nf, nb = len(self.f32), len(self.bf)
        fs, bs = self.f32[i % nf], self.bf[i % nb]
        if i >= nf:
            p.wait("sync", self.csem, self.conv_hist[i - nf])
        dsem = "s_w%d" % (i % self.nsem)
        if src2 is None:
            ld = p.dma(fs[:, 0:cols], src_ap, sem=dsem)
        else:
            p.dma(fs[0:64, 0:cols], src_ap, sem=dsem)
            ld = p.dma(fs[64:128, 0:cols], src2, sem=dsem)
        p.wait_t(self.conv, ld)
        if i >= nb:
            p.wait_t(self.conv, self.xp_hist[i - nb])
        if self.conv == "scalar":
            cv = p.op("scalar", lambda s, o=bs[:, 0:cols], i_=fs[:, 0:cols]:
                      s.copy(o, i_), inc="s_act")
        else:
            cv = p.op("vector", lambda v, o=bs[:, 0:cols], i_=fs[:, 0:cols]:
                      v.tensor_copy(o, i_), inc="s_dve")
        self.conv_hist[i] = cv
        p.wait("sync", self.csem, cv)
        p.wait("sync", "s_pe", war_val)
        xv = p.xp(dst_ap, bs[:, 0:cols])
        self.xp_hist[i] = xv
        return xv


def build_nc():
    nc = bass.Bass("TRN2", target_bir_lowering=False)

    xb = nc.dram_tensor("xb", [LBATCH, D], F32, kind="ExternalInput").ap()
    wq = nc.dram_tensor("wq", [D, D], F32, kind="ExternalInput").ap()
    wk = nc.dram_tensor("wk", [512, D], F32, kind="ExternalInput").ap()
    wv = nc.dram_tensor("wv", [512, D], F32, kind="ExternalInput").ap()
    wo = nc.dram_tensor("wo", [D, D], F32, kind="ExternalInput").ap()
    wg = nc.dram_tensor("wg", [NE, D], F32, kind="ExternalInput").ap()
    w1 = nc.dram_tensor("w1", [NE, HF, D], F32, kind="ExternalInput").ap()
    w2 = nc.dram_tensor("w2", [NE, HF, D], F32, kind="ExternalInput").ap()
    w3 = nc.dram_tensor("w3", [NE, D, HF], F32, kind="ExternalInput").ap()
    cos_t = nc.dram_tensor("cos_t", [128, NTL, HD], BF16, kind="ExternalInput").ap()
    sin_t = nc.dram_tensor("sin_t", [128, NTL, HD], BF16, kind="ExternalInput").ap()
    triu_c = nc.dram_tensor("triu_c", [128, 128], BF16, kind="ExternalInput").ap()
    route_c = nc.dram_tensor("route_c", [128, 16], F32, kind="ExternalInput").ap()
    tok_c = nc.dram_tensor("tok_c", [128, 16], mybir.dt.int32, kind="ExternalInput").ap()
    out = nc.dram_tensor("out", [TOWN, D], F32, kind="ExternalOutput").ap()
    xn_scr = nc.dram_tensor("xn_scr", [1152, D], BF16, kind="Internal").ap()
    w_scr = nc.dram_tensor("w_scr", [1152, NE], F32, kind="Internal").ap()
    elist = nc.dram_tensor("elist", [NE * 640, 2], mybir.dt.int32, kind="Internal").ap()
    y_scr = nc.dram_tensor("y_scr", [1152, D], BF16, kind="Internal").ap()
    CAP = 384

    ctx = contextlib.ExitStack()
    with ctx:
        sems = {}
        for name in DMA_SEMS + ("s_pe", "s_act", "s_dve"):
            sems[name] = ctx.enter_context(nc.semaphore(name))
        p = Pl(sems)

        consts = ctx.enter_context(nc.sbuf_tensor("consts", [128, 16], F32))
        xn_own = ctx.enter_context(nc.sbuf_tensor("xn_own", [128, NDC, TOWN], BF16))
        g8 = ctx.enter_context(nc.sbuf_tensor("g8", [128, NTO, NE], F32))
        wgate = ctx.enter_context(nc.sbuf_tensor("wgate", [128, NTO, NE], F32))
        wgt = ctx.enter_context(nc.sbuf_tensor("wgt", [128, NDC, 16], BF16))
        triu_s = ctx.enter_context(nc.sbuf_tensor("triu_s", [128, 128], BF16))
        route_s = ctx.enter_context(nc.sbuf_tensor("route_s", [128, 16], F32))
        tok_s = ctx.enter_context(nc.sbuf_tensor("tok_s", [128, 16], mybir.dt.int32))
        selm = ctx.enter_context(nc.sbuf_tensor("selm", [128, NTO, NE], BF16))
        ridx = ctx.enter_context(nc.sbuf_tensor("ridx", [128, NTO, NE], F32))
        ridxu = ctx.enter_context(nc.sbuf_tensor("ridxu", [128, NTO, NE], mybir.dt.uint32))
        rtot = ctx.enter_context(nc.sbuf_tensor("rtot", [128, 80], F32))
        ones128r = ctx.enter_context(nc.sbuf_tensor("ones128r", [1, 128], F32))
        ones128c = ctx.enter_context(nc.sbuf_tensor("ones128c", [128, 1], BF16))
        pfill = ctx.enter_context(nc.sbuf_tensor("pfill", [128, 80], mybir.dt.int32))
        sA = ctx.enter_context(nc.sbuf_tensor("sA", [128, NDC, 512], BF16))
        sB = ctx.enter_context(nc.sbuf_tensor("sB", [128, NDC, 512], BF16))
        pbig = ctx.enter_context(nc.psum_tensor("pbig", [128, 4096], F32))

        def bank(i, w=512):
            return pbig[:, i * 512:i * 512 + w]

        rstack = contextlib.ExitStack()
        K_rt = rstack.enter_context(nc.sbuf_tensor("K_rt", [128, 4, LBATCH], BF16, side="right"))
        V_aug = rstack.enter_context(nc.sbuf_tensor("V_aug", [128, NTL, 520], BF16, side="right"))

        astack2 = contextlib.ExitStack()
        cos_s = astack2.enter_context(nc.sbuf_tensor("cos_s", [128, NTL, HD], BF16))
        sin_s = astack2.enter_context(nc.sbuf_tensor("sin_s", [128, NTL, HD], BF16))
        scrA = astack2.enter_context(nc.sbuf_tensor("scrA", [128, 512], F32))
        scrB = astack2.enter_context(nc.sbuf_tensor("scrB", [128, 512], F32))
        stats = astack2.enter_context(nc.sbuf_tensor("stats", [128, 8], F32))
        Q_tm = astack2.enter_context(nc.sbuf_tensor("Q_tm", [128, 2, 512], BF16))
        bigA = astack2.enter_context(nc.sbuf_tensor("bigA", [128, 4, D], BF16))
        astack1 = contextlib.ExitStack()
        xbuf = astack1.enter_context(nc.sbuf_tensor("xbuf", [128, 2, D], F32))
        xn_tm = astack1.enter_context(nc.sbuf_tensor("xn_tm", [128, 2, D], BF16))
        xnf = astack1.enter_context(nc.sbuf_tensor("xnf", [128, 3, NDC, 128], BF16))
        K_tm = astack1.enter_context(nc.sbuf_tensor("K_tm", [128, 4, 512], BF16))

        cvals = [0.125, EXP_BIAS, LN_EPS, RECIP_D, 1.0, -1.0, BIG, -BIG, 0.5]
        for i, v in enumerate(cvals):
            nc.const_aps.register(F32, v, consts[:, i:i + 1])

        def vmemsets(v):
            for i, val in enumerate(cvals):
                v.memset(consts[:, i:i + 1], val)
            return v.memset(
                V_aug[:, :, :].rearrange("p l (h c) -> p l h c", c=65)[:, :, :, 64:65],
                1.0)
        p.op("vector", vmemsets, inc="s_dve")
        for e in ("scalar", "tensor", "sync"):
            p.wait(e, "s_dve", 1)

        p.dma(cos_s[:, :, :], cos_t, sem="s_y")
        v_tab = p.dma(sin_s[:, :, :], sin_t, sem="s_y")
        p.wait_t("vector", v_tab)
        p.dma(triu_s[:, :], triu_c, sem="s_y")
        p.dma(route_s[:, :], route_c, sem="s_y")
        rc_tab = p.dma(tok_s[:, :], tok_c, sem="s_y")
        mz1 = p.op("vector", lambda v: v.memset(ones128r[0:1, :], 1.0))
        p.op("vector", lambda v: v.memset(ones128c[:, :], 1.0))
        mz2 = p.op("vector", lambda v: v.memset(pfill[:, :], 1024))
        mz3 = p.op("vector", lambda v: v.memset(xn_tm[:, 0, :], 0.0))
        mz4 = p.op("vector", lambda v: v.memset(xbuf[:, 0, :], 0.0))
        p.wait("sync", "s_dve", mz4)
        p.dma(xn_scr[1024:1152, :], xn_tm[:, 0, :], sem="s_y")
        xz16 = xbuf[:, 0, :].bitcast(BF16)
        for zi in range(9):
            p.dma(y_scr[zi * 128:(zi + 1) * 128, :], xz16[:, 0:D], sem="s_y")
        zfin = p.dma(elist.rearrange("(p a) o -> p (a o)", p=128), pfill[:, :], sem="s_y")

        wsA = WStream(p, bigA, 4, "scalar")
        for b in range(4):
            wk_ready = wsA.chunk(wk[b * 128:(b + 1) * 128, :],
                                 sA[:, :, b * 128:(b + 1) * 128], D)
        for b in range(4):
            wv_ready = wsA.chunk(wv[b * 128:(b + 1) * 128, :],
                                 sB[:, :, b * 128:(b + 1) * 128], D)
        wgslot = wsA.bf[0]
        p.wait_t("vector", wsA.xp_hist[wsA.i - 2])
        ms = p.op("vector", lambda v: v.memset(wgslot[0:16, :], 0.0), inc="s_dve")
        p.wait("sync", "s_dve", ms)
        p.wait("sync", "s_act", wsA.conv_hist[wsA.i - 1])
        ldg = p.dma(wsA.f32[0][0:8, :], wg, sem="s_w0")
        p.wait_t("scalar", ldg)
        p.wait("scalar", "s_dve", ms)
        cvg = p.op("scalar", lambda s: s.copy(wgslot[0:8, :], wsA.f32[0][0:8, :]),
                   inc="s_act")
        wsA.conv_hist[wsA.i] = cvg
        wsA.i += 1
        p.wait("sync", "s_act", cvg)
        wg_ready = p.xp(wgt[:, :, :], wgslot[0:16, :])
        wsA.xp_hist[wsA.i - 1] = wg_ready

        psV = [bank(0), bank(1)]
        psK = [bank(2), bank(3)]
        psG = [bank(4), bank(5)]

        ld_done = [0] * NTL
        sq_done = [0] * NTL
        var_done = [0] * NTL
        ln_done = [0] * NTL
        xp_done = [0] * NTL
        kv_done = [0] * NTL
        ev_done = [0] * NTL
        kxp_hist = [0] * NTL

        p.wait("sync", "s_y", zfin[1])
        p.wait("scalar", "s_y", zfin[1])
        xn_store = [0] * NTO
        for g in range(2):
            ld_done[g] = p.dma(xbuf[:, g % 2, :], xb[g * 128:(g + 1) * 128, :],
                               sem="s_x%d" % (g % 2))

        def emit_evac(gg):
            own_ = gg < NTO
            p.wait("vector", "s_pe", kv_done[gg])
            if gg >= 4:
                p.wait_t("vector", kxp_hist[gg - 4])
            pr_ = gg % 2
            ps_ = psK[pr_].rearrange("p (h c) -> p h c", c=64)
            cosb_ = cos_s[:, gg, :].unsqueeze(1).broadcast_to([128, 8, 64])
            sinb_ = sin_s[:, gg, :].unsqueeze(1).broadcast_to([128, 8, 64])
            t1_ = scrA[:, :].rearrange("p (h c) -> p h c", c=64)
            t2_ = scrB[:, :].rearrange("p (h c) -> p h c", c=64)
            p.op("vector", lambda v, g=gg, pr=pr_: v.tensor_copy(
                V_aug[:, g, :].rearrange("p (h c) -> p h c", c=65)[:, :, 0:64],
                psV[pr].rearrange("p (h c) -> p h c", c=64)))
            p.op("vector", lambda v, ps=ps_, cosb=cosb_, t1=t1_:
                 v.tensor_tensor(t1, ps, cosb, ALU.mult))
            p.op("vector", lambda v, ps=ps_, sinb=sinb_, t2=t2_:
                 v.tensor_tensor(t2[:, :, 0:32], ps[:, :, 32:64],
                                 sinb[:, :, 0:32], ALU.mult))
            p.op("vector", lambda v, ps=ps_, sinb=sinb_, t2=t2_:
                 v.tensor_tensor(t2[:, :, 32:64], ps[:, :, 0:32],
                                 sinb[:, :, 32:64], ALU.mult))
            ev_done[gg] = p.op("vector", lambda v, g=gg: v.tensor_tensor(
                K_tm[:, g % 4, :], scrA[:, :], scrB[:, :], ALU.add))
            if own_:
                ev_done[gg] = p.op("vector", lambda v, g=gg, pr=pr_: v.tensor_copy(
                    g8[:, g, :], psG[pr][:, 0:8]))
            p.wait("sync", "s_dve", ev_done[gg])
            kxp_hist[gg] = p.xp(K_rt[:, :, gg * 128:(gg + 1) * 128],
                                K_tm[:, gg % 4, :])

        for g in range(NTL):
            sl = g % 2
            own = g < NTO

            p.wait_t("scalar", ld_done[g])
            if g >= 2:
                p.wait_t("scalar", xp_done[g - 2])
                if g - 2 < NTO:
                    p.wait_t("scalar", xn_store[g - 2])
            if g >= 1:
                p.wait("scalar", "s_dve", var_done[g - 1])
            sq_done[g] = p.op("scalar", lambda s, sl=sl: s.activation(
                xn_tm[:, sl, :], xbuf[:, sl, :], AF.Square,
                accum_out=stats[:, 1:2]), inc="s_act")

            p.wait_t("vector", ld_done[g])

            p.op("vector", lambda v, sl=sl: v.reduce_sum(
                stats[:, 0:1], xbuf[:, sl, :], axis=AX))
            p.op("vector", lambda v: v.tensor_scalar(
                stats[:, 2:3], stats[:, 0:1], RECIP_D, None, ALU.mult))
            p.wait("vector", "s_act", sq_done[g])

            p.op("vector", lambda v: v.tensor_scalar(
                stats[:, 3:4], stats[:, 1:2], RECIP_D, None, ALU.mult))
            p.op("vector", lambda v: v.tensor_tensor(
                stats[:, 4:5], stats[:, 2:3], stats[:, 2:3], ALU.mult))
            var_done[g] = p.op("vector", lambda v: v.tensor_tensor(
                stats[:, 3:4], stats[:, 3:4], stats[:, 4:5], ALU.subtract))
            p.wait("scalar", "s_dve", var_done[g])
            sr = p.op("scalar", lambda s: s.activation(
                stats[:, 5:6], stats[:, 3:4], AF.Sqrt, bias=LN_EPS), inc="s_act")
            p.wait("vector", "s_act", sr)

            p.op("vector", lambda v: v.reciprocal(stats[:, 6:7], stats[:, 5:6]))
            p.op("vector", lambda v: v.tensor_scalar(
                stats[:, 7:8], stats[:, 2:3], stats[:, 6:7], -1.0,
                ALU.mult, ALU.mult))
            ln_done[g] = p.op("vector", lambda v, sl=sl: v.tensor_scalar(
                xn_tm[:, sl, :], xbuf[:, sl, :], stats[:, 6:7], stats[:, 7:8],
                ALU.mult, ALU.add))

            p.wait("sync", "s_dve", ln_done[g])
            if own:
                xn_store[g] = p.dma(xn_scr[g * 128:(g + 1) * 128, :],
                                    xn_tm[:, sl, :], sem="s_y")
                dst = xn_own[:, :, g * 128:(g + 1) * 128]
            else:
                if g >= NTO + 3:
                    p.wait("sync", "s_pe", kv_done[g - 3])
                dst = xnf[:, g % 3, :, :]
            xp_done[g] = p.xp(dst, xn_tm[:, sl, :])

            p.wait_t("tensor", xp_done[g])
            p.wait_t("tensor", wv_ready)
            p.wait_t("tensor", wg_ready)
            if g >= 2:
                p.wait("tensor", "s_dve", ev_done[g - 2])

            def kvg_mms(t, g=g, own=own):
                pr = g % 2
                last = None
                for dc in range(NDC):
                    if own:
                        lhsT = xn_own[:, dc, g * 128:(g + 1) * 128]
                    else:
                        lhsT = xnf[:, g % 3, dc, :]
                    st, sp = dc == 0, dc == NDC - 1
                    t.matmul(psV[pr], lhsT, sB[:, dc, :], start=st, stop=sp)
                    last = t.matmul(psK[pr], lhsT, sA[:, dc, :], start=st, stop=sp)
                    if own:
                        last = t.matmul(psG[pr][:, 0:8], lhsT, wgt[:, dc, 0:8],
                                        start=st, stop=sp)
                return last
            kv_done[g] = p.op("tensor", kvg_mms, inc="s_pe")

            if g >= 1:
                emit_evac(g - 1)
            if g + 2 < NTL:
                p.wait("sync", "s_dve", ln_done[g])
                p.wait("sync", "s_act", sq_done[g])
                ld_done[g + 2] = p.dma(xbuf[:, g % 2, :],
                                       xb[(g + 2) * 128:(g + 3) * 128, :],
                                       sem="s_x%d" % (g % 2))

        emit_evac(NTL - 1)

        astack1.close()
        Qp = rstack.enter_context(nc.sbuf_tensor("Qp", [128, NDC, TOWN], BF16, side="right"))
        qstg2 = astack2.enter_context(nc.sbuf_tensor("qstg2", [128, D], F32))
        a1_pe = p.cnt["s_pe"]

        gv = 0
        for tc in range(NTO):
            gg = g8[:, tc, :]
            m1, m2 = stats[:, 0:1], stats[:, 1:2]
            p.op("vector", lambda v, gg=gg, m1=m1: v.reduce_max(m1, gg, axis=AX))
            p.op("vector", lambda v, gg=gg, m1=m1, tc=tc: v.tensor_scalar(
                wgate[:, tc, :], gg, m1, -BIG, ALU.is_ge, ALU.mult))
            p.op("vector", lambda v, gg=gg, tc=tc: v.tensor_tensor(
                wgate[:, tc, :], gg, wgate[:, tc, :], ALU.add))
            p.op("vector", lambda v, m2=m2, tc=tc: v.reduce_max(
                m2, wgate[:, tc, :], axis=AX))
            p.op("vector", lambda v, gg=gg, m2=m2, tc=tc: v.tensor_scalar(
                wgate[:, tc, :], gg, m2, None, ALU.is_ge))
            p.op("vector", lambda v, m1=m1: v.tensor_scalar(
                stats[:, 2:3], m1, -1.0, None, ALU.mult))
            gv = p.op("vector", lambda v, gg=gg: v.tensor_scalar(
                gg, gg, stats[:, 2:3], None, ALU.add))
        p.wait("scalar", "s_dve", gv)

        def gate_exp(s):
            last = None
            for tc in range(NTO):
                last = s.activation(g8[:, tc, :], g8[:, tc, :], AF.Exp)
            return last
        ge = p.op("scalar", gate_exp, inc="s_act")
        p.wait("vector", "s_act", ge)

        for tc in range(NTO):
            p.op("vector", lambda v, tc=tc: v.tensor_tensor(
                g8[:, tc, :], g8[:, tc, :], wgate[:, tc, :], ALU.mult))
            p.op("vector", lambda v, tc=tc: v.reduce_sum(
                stats[:, 3:4], g8[:, tc, :], axis=AX))
            p.op("vector", lambda v: v.reciprocal(stats[:, 4:5], stats[:, 3:4]))
            p.op("vector", lambda v, tc=tc: v.tensor_scalar(
                wgate[:, tc, :], g8[:, tc, :], stats[:, 4:5], None, ALU.mult))

        psR = bank(2)
        psR2 = bank(3)
        for tc in range(NTO):
            p.op("vector", lambda v, tc=tc: v.tensor_scalar(
                selm[:, tc, :], wgate[:, tc, :], 0.0, None, ALU.is_gt))
        cum_ev = [0] * NTO
        for tc in range(NTO):
            sv = p.cnt["s_dve"]
            p.wait("tensor", "s_dve", sv)
            if tc >= 1:
                p.wait("tensor", "s_dve", cum_ev[tc - 1])
            mmv = p.op("tensor", lambda t, tc=tc: t.matmul(
                psR[:, 0:8], triu_s[:, :], selm[:, tc, :],
                start=True, stop=True), inc="s_pe")
            mmt = p.op("tensor", lambda t, tc=tc: t.matmul(
                psR2[0:1, 0:8], ones128c[:, :], selm[:, tc, :],
                start=True, stop=True), inc="s_pe")
            p.wait("vector", "s_pe", mmt)
            p.op("vector", lambda v, tc=tc: v.tensor_copy(
                ridx[:, tc, :], psR[:, 0:8]))
            cum_ev[tc] = p.op("vector", lambda v, tc=tc: v.tensor_copy(
                rtot[0:1, tc * 8:tc * 8 + 8], psR2[0:1, 0:8]))
        p.op("vector", lambda v: v.memset(rtot[0:1, 64:72], 0.0))
        for tc in range(1, NTO):
            p.op("vector", lambda v, tc=tc: v.tensor_tensor(
                rtot[0:1, 72:80], rtot[0:1, 64:72],
                rtot[0:1, (tc - 1) * 8:tc * 8], ALU.add))
            p.op("vector", lambda v: v.tensor_copy(rtot[0:1, 64:72],
                                                   rtot[0:1, 72:80]))
            ro = p.cnt["s_dve"]
            p.wait("tensor", "s_dve", ro)
            rr = p.op("tensor", lambda t, tc=tc: t.matmul(
                psR2[:, 8:16], ones128r[0:1, :], rtot[0:1, 64:72],
                start=True, stop=True), inc="s_pe")
            p.wait("vector", "s_pe", rr)
            p.op("vector", lambda v, tc=tc: v.tensor_tensor(
                ridx[:, tc, :], ridx[:, tc, :], psR2[:, 8:16], ALU.add))
        for tc in range(NTO):
            p.op("vector", lambda v, tc=tc: v.tensor_scalar(
                ridx[:, tc, :], ridx[:, tc, :], -1.0, None, ALU.add))
            p.op("vector", lambda v, tc=tc: v.tensor_tensor(
                ridx[:, tc, :], ridx[:, tc, :], selm[:, tc, :], ALU.mult))
            p.op("vector", lambda v, tc=tc: v.tensor_scalar(
                rtot[:, 0:8], selm[:, tc, :], -1.0, 1.0, ALU.mult, ALU.add))
            p.op("vector", lambda v, tc=tc: v.tensor_scalar(
                rtot[:, 0:8], rtot[:, 0:8], route_s[:, 8:9], None, ALU.mult))
            p.op("vector", lambda v, tc=tc: v.tensor_tensor(
                ridx[:, tc, :], ridx[:, tc, :], rtot[:, 0:8], ALU.add))
            p.op("vector", lambda v, tc=tc: v.tensor_tensor(
                ridx[:, tc, :], ridx[:, tc, :], route_s[:, 0:8], ALU.add))
            p.op("vector", lambda v, tc=tc: v.tensor_copy(
                ridxu[:, tc, :], ridx[:, tc, :]))
        rfin = p.cnt["s_dve"]
        p.wait("sync", "s_dve", rfin)
        p.dma(w_scr[0:TOWN, :].rearrange("(a p) e -> p a e", p=128),
              wgate[:, :, :], sem="s_y")
        p.op("vector", lambda v: v.memset(rtot[:, 0:8], 0.0))
        p.wait("sync", "s_dve", p.cnt["s_dve"])
        p.dma(w_scr[1024:1152, :], rtot[:, 0:8], sem="s_y")
        p.wait("sync", "s_y", zfin[1])
        p.wait("gpsimd", "s_y", zfin[1])
        p.wait("gpsimd", "s_dve", rfin)
        for e in range(NE):
            for tc in range(NTO):
                rt = p.gdma(lambda g, e=e, tc=tc:
                            g.indirect_dma_start(
                                elist,
                                bass.IndirectOffsetOnAxis(
                                    ap=ridxu[:, tc, e:e + 1], axis=0),
                                tok_s[:, 2 * tc:2 * tc + 2], None))
        route_done = [rt]

        psQ = [bank(0), bank(1)]
        q_ev = [0] * (4 * NTO)
        q_xp = [0] * (4 * NTO)
        qmm_by_slot = [a1_pe, a1_pe]
        p.wait("sync", "s_pe", a1_pe)
        p.wait("sync", "s_xp", p.cnt["s_xp"])
        wsA.f32.append(qstg2[:, :])
        wsA.srcs = [None] * wsA.i
        for es in range(4):
            for b in range(4):
                c = es * 4 + b
                htop = (c // 4) * 8 + (c % 4)
                wsA.srcs.append((wq[htop * 64:(htop + 1) * 64, :], D,
                                 wq[(htop + 4) * 64:(htop + 5) * 64, :]))
        for es in range(4):
            slot = sA if es % 2 == 0 else sB
            last_xp = 0
            for b in range(4):
                last_xp = wsA.chunk2(
                    slot[:, :, b * 128:(b + 1) * 128],
                    war_val=qmm_by_slot[es % 2])
            p.wait_t("tensor", last_xp)
            last_mm = 0
            for tc in range(NTO):
                fl = es * NTO + tc
                pr = fl % 2
                if fl >= 2:
                    p.wait("tensor", "s_dve", q_ev[fl - 2])
                else:
                    p.wait("tensor", "s_dve", ev_done[NTL - 1])

                def q_mms(t, tc=tc, slot=slot, pr=pr):
                    last = None
                    for dc in range(NDC):
                        last = t.matmul(psQ[pr],
                                        xn_own[:, dc, tc * 128:(tc + 1) * 128],
                                        slot[:, dc, :],
                                        start=(dc == 0), stop=(dc == NDC - 1))
                    return last
                last_mm = p.op("tensor", q_mms, inc="s_pe")

                p.wait("vector", "s_pe", last_mm)
                if fl >= 2:
                    p.wait_t("vector", q_xp[fl - 2])

                ps_ = psQ[pr].rearrange("p (h c) -> p h c", c=64)
                cosb_ = cos_s[:, tc, :].unsqueeze(1).broadcast_to([128, 8, 64])
                sinb_ = sin_s[:, tc, :].unsqueeze(1).broadcast_to([128, 8, 64])
                t1_ = scrA[:, :].rearrange("p (h c) -> p h c", c=64)
                t2_ = scrB[:, :].rearrange("p (h c) -> p h c", c=64)
                p.op("vector", lambda v, ps=ps_, cosb=cosb_, t1=t1_:
                     v.tensor_tensor(t1, ps, cosb, ALU.mult))
                p.op("vector", lambda v, ps=ps_, sinb=sinb_, t2=t2_:
                     v.tensor_tensor(t2[:, :, 0:32], ps[:, :, 32:64],
                                     sinb[:, :, 0:32], ALU.mult))
                p.op("vector", lambda v, ps=ps_, sinb=sinb_, t2=t2_:
                     v.tensor_tensor(t2[:, :, 32:64], ps[:, :, 0:32],
                                     sinb[:, :, 32:64], ALU.mult))
                q_ev[fl] = p.op("vector", lambda v, fl=fl: v.tensor_tensor(
                    Q_tm[:, fl % 2, :], scrA[:, :], scrB[:, :], ALU.add))
                p.wait("sync", "s_dve", q_ev[fl])
                q_xp[fl] = p.xp(
                    Qp[:, es * 4:(es + 1) * 4, tc * 128:(tc + 1) * 128],
                    Q_tm[:, fl % 2, :])
            qmm_by_slot[es % 2] = last_mm
        qmm_done = max(qmm_by_slot)

        p.barrier(exclude=("s_ca", "s_cb", "s_cc"))
        astack2.close()
        bstack = contextlib.ExitStack()
        attn = bstack.enter_context(nc.sbuf_tensor("attn", [128, NDC, TOWN], BF16))
        Ebuf = bstack.enter_context(nc.sbuf_tensor("Ebuf", [128, 3, 1024], BF16))
        rrow = bstack.enter_context(nc.sbuf_tensor("rrow", [128, 512], F32))
        nrm = bstack.enter_context(nc.sbuf_tensor("nrm", [128, 512], BF16))
        ones64 = bstack.enter_context(nc.sbuf_tensor("ones64", [1, 64], F32))
        ov = p.op("vector", lambda v: v.memset(ones64[0:1, :], 1.0), inc="s_dve")
        p.wait("tensor", "s_dve", ov)

        psS = [pbig[:, 0:1024], pbig[:, 1024:2048], pbig[:, 2048:3072]]
        psPV = [pbig[:, 3072:3584], pbig[:, 3584:4096]]

        GROUPS = [(2 * i, 2 * i + 2) for i in range(16)]
        exp_hist = []
        pv_hist = []
        mult_hist = []
        sBflat = sB[:, :, :].rearrange("p a b -> p (a b)")
        wo_pref = {}
        wo_war = {}

        def prefetch_wo_ec0():
            last = 0
            prev_cv = 0
            prev_xp = 0
            for b in range(4):
                f32v = sBflat[:, 4096:8192].bitcast(F32)
                if prev_cv:
                    p.wait("sync", "s_dve", prev_cv)
                ld = p.dma(f32v, wo[b * 128:(b + 1) * 128, :], sem="s_w0")
                p.wait_t("vector", ld)
                if prev_xp:
                    p.wait_t("vector", prev_xp)
                cv = p.op("vector", lambda v, o=sBflat[:, 0:2048], i_=f32v:
                          v.tensor_copy(o, i_), inc="s_dve")
                prev_cv = cv
                p.wait("sync", "s_dve", cv)
                p.wait("sync", "s_pe", qmm_by_slot[0])
                last = p.xp(sA[:, :, b * 128:(b + 1) * 128], sBflat[:, 0:2048])
                prev_xp = last
            wo_pref[0] = last

        prefetch_wo_ec0()

        rb_borrow = {0: 0, 1: 0, 2: 0}
        gabs_ctr = [0]

        def emit_s(gabs, c, st2_, lc):
            kc = c // 4
            qoff = st2_ * 512
            sl = gabs % 3
            if gabs >= 3:
                p.wait("tensor", "s_act", exp_hist[gabs - 3])
            p.wait("tensor", "s_dve", rb_borrow[sl])

            def s_mms(t, sl=sl, kc=kc, c=c, qoff=qoff, lc=lc):
                t.matmul(psS[sl][:, 0:512],
                         K_rt[0:64, kc, lc * 128:(lc + 1) * 128],
                         Qp[0:64, c, qoff:qoff + 512],
                         start=True, stop=True)
                return t.matmul(psS[sl][:, 512:1024],
                                K_rt[64:128, kc, lc * 128:(lc + 1) * 128],
                                Qp[64:128, c, qoff:qoff + 512],
                                start=True, stop=True)
            sv = p.op("tensor", s_mms, inc="s_pe")
            p.wait("scalar", "s_pe", sv)
            if gabs >= 3:
                p.wait("scalar", "s_pe", pv_hist[gabs - 3])
            ev = p.op("scalar", lambda s, sl=sl: s.activation(
                Ebuf[:, sl, :], psS[sl][:, :], AF.Exp,
                bias=EXP_BIAS, scale=0.125), inc="s_act")
            exp_hist.append(ev)

        def emit_pv(gabs, c, lc):
            kv_e = (c // 4) * 2
            sl = gabs % 3
            p.wait("tensor", "s_act", exp_hist[gabs])
            if lc == 0 and len(mult_hist) >= 2:
                p.wait("tensor", "s_dve", mult_hist[-1])

            def pv_mms(t, sl=sl, kv_e=kv_e, lc=lc):
                t.matmul(psPV[0][0:65, :],
                         V_aug[:, lc, kv_e * 65:kv_e * 65 + 65],
                         Ebuf[:, sl, 0:512],
                         start=(lc == 0), stop=(lc == NTL - 1))
                return t.matmul(psPV[1][0:65, :],
                                V_aug[:, lc, (kv_e + 1) * 65:(kv_e + 1) * 65 + 65],
                                Ebuf[:, sl, 512:1024],
                                start=(lc == 0), stop=(lc == NTL - 1))
            pv_hist.append(p.op("tensor", pv_mms, inc="s_pe"))

        for c in range(16):
            he = (c // 4) * 8 + (c % 4)
            for st2 in range(2):
                qoff = st2 * 512
                g0 = gabs_ctr[0]
                for lc in range(NTL):
                    emit_s(g0 + lc, c, st2, lc)
                    if lc >= 2:
                        emit_pv(g0 + lc - 2, c, lc - 2)
                emit_pv(g0 + NTL - 2, c, NTL - 2)
                emit_pv(g0 + NTL - 1, c, NTL - 1)
                gabs_ctr[0] = g0 + NTL

                sl_last = (g0 + NTL - 1) % 3
                p.wait("vector", "s_pe", pv_hist[-1])
                for j, h in ((0, he), (1, he + 4)):
                    ec, qrow = h // 2, 64 * (h % 2)
                    rbt = psS[sl_last][:, j * 512:(j + 1) * 512]
                    rc = p.op("vector", lambda v, j=j: v.reciprocal(
                        rrow[0:1, :], psPV[j][64:65, :]))
                    p.wait("tensor", "s_dve", rc)
                    rb = p.op("tensor", lambda t, rbt=rbt: t.matmul(
                        rbt[0:64, :], ones64[0:1, :], rrow[0:1, :],
                        start=True, stop=True), inc="s_pe")
                    p.wait("vector", "s_pe", rb)
                    rbc = p.op("vector", lambda v, rbt=rbt: v.tensor_copy(
                        rrow[0:64, :], rbt[0:64, :]))
                    rb_borrow[sl_last] = rbc
                    if qrow == 0:
                        mh = p.op("vector", lambda v, ec=ec, qoff=qoff, j=j:
                                  v.tensor_tensor(attn[0:64, ec, qoff:qoff + 512],
                                                  psPV[j][0:64, :], rrow[0:64, :],
                                                  ALU.mult))
                    else:
                        p.op("vector", lambda v, j=j: v.tensor_tensor(
                            nrm[0:64, :], psPV[j][0:64, :], rrow[0:64, :],
                            ALU.mult))
                        mh = p.op("vector", lambda v, ec=ec, qoff=qoff:
                                  v.tensor_copy(attn[64:128, ec, qoff:qoff + 512],
                                                nrm[0:64, :]))
                    mult_hist.append(mh)

        p.barrier(exclude=("s_ca", "s_cb", "s_cc"))
        rstack.close()
        cstack = contextlib.ExitStack()
        yb = cstack.enter_context(nc.sbuf_tensor("yb", [128, NTO, D], F32, side="right"))
        c2stack = contextlib.ExitStack()
        cstage = c2stack.enter_context(nc.sbuf_tensor("cstage", [128, 8, D], BF16, side="right"))
        wsC = WStream(p, cstage, 8, "scalar")
        psY = [bank(0), bank(1)]

        y_init = p.dma(yb[:, :, :],
                       xb[0:TOWN, :].rearrange("(tc p) d -> p tc d", p=128),
                       sem="s_y")

        c_ev = [0, 0]
        last_c_mm = 0
        wo_war = {0: qmm_by_slot[1], 1: qmm_by_slot[1]}
        for ec in range(4):
            slot = sA if ec % 2 == 0 else sB
            if ec >= 1:
                for b in range(4):
                    eb = ec * 4 + b
                    wo_pref[ec] = wsC.chunk(wo[eb * 128:(eb + 1) * 128, :],
                                            slot[:, :, b * 128:(b + 1) * 128], D,
                                            war_val=wo_war[ec % 2])
            p.wait_t("tensor", wo_pref[ec])
            for tc in range(NTO):
                pr = tc % 2
                p.wait("tensor", "s_dve", c_ev[pr])

                def wo_mms(t, tc=tc, slot=slot, pr=pr):
                    last = None
                    for dc in range(NDC):
                        last = t.matmul(psY[pr],
                                        attn[:, dc, tc * 128:(tc + 1) * 128],
                                        slot[:, dc, :],
                                        start=(dc == 0), stop=(dc == NDC - 1))
                    return last
                mv = p.op("tensor", wo_mms, inc="s_pe")
                last_c_mm = mv
                p.wait("vector", "s_pe", mv)
                p.wait_t("vector", y_init)
                c_ev[pr] = p.op("vector", lambda v, tc=tc, ec=ec, pr=pr:
                                v.tensor_tensor(
                                    yb[:, tc, ec * 512:(ec + 1) * 512], psY[pr],
                                    yb[:, tc, ec * 512:(ec + 1) * 512], ALU.add),
                                inc="s_dve")
            wo_war[ec % 2] = last_c_mm

        p.barrier(exclude=("s_ca", "s_cb", "s_cc"))
        bstack.close()
        c2stack.close()
        dstack = contextlib.ExitStack()
        hprime = dstack.enter_context(nc.sbuf_tensor("hprime", [128, NHC, CAP], BF16))
        aprime = dstack.enter_context(nc.sbuf_tensor("aprime", [128, 512], BF16))
        bigD = dstack.enter_context(nc.sbuf_tensor("bigD", [128, 13824], BF16))
        xe_tm = dstack.enter_context(nc.sbuf_tensor("xe_tm", [128, D], BF16))
        xe_fm = dstack.enter_context(nc.sbuf_tensor("xe_fm", [128, NDC, CAP], BF16))
        ybuf = dstack.enter_context(nc.sbuf_tensor("ybuf", [128, 3, D], BF16))
        eids = dstack.enter_context(nc.sbuf_tensor("eids", [128, 2, 8], mybir.dt.int32))
        wloc = dstack.enter_context(nc.sbuf_tensor("wloc", [128, 2, 24], F32))
        wcol = dstack.enter_context(nc.sbuf_tensor("wcol", [128, 2, 4], F32))
        wstg2 = dstack.enter_context(nc.sbuf_tensor("wstg2", [128, D], F32))
        wsD = WStream(p, bigD[:, 0:8192].rearrange("p (a b) -> p a b", b=D),
                      4, "vector", extra_f32=[wstg2[:, :]], nsem=4)

        wsl2 = xn_own[:, :, 0:512]
        w3r2 = xn_own[:, 0:NHC, 512:1024]
        psAB = [bank(0), bank(1), bank(2), bank(3)]
        w1_slots = [sA[:, :, 0:128], sA[:, :, 128:256],
                    wsl2[:, :, 0:128], wsl2[:, :, 128:256]]
        w2_slots = [sA[:, :, 256:384], sA[:, :, 384:512],
                    wsl2[:, :, 256:384], wsl2[:, :, 384:512]]
        w3_ring1 = bigD[:, 8192:8192 + NHC * 512] \
            .rearrange("p (h c) -> p h c", c=512)
        w3_slots = [sB[:, 0:NHC, 0:512], w3_ring1, w3r2]

        hb_ctr = [0]
        w3_ctr = [0]
        w12_war = [last_c_mm] * 4
        w3_war = [last_c_mm] * 3
        hp_done = [0] * NHC
        ab_free = [0, 0, 0, 0]
        yv_done = [0, 0, 0, 0]
        hp_war = [0]
        w12_xp = {}
        w3_xp = {}
        gath_xp = {}
        gath_w = {}
        scat_done = {0: None}

        def emit_w12_chunks(e, hb):
            r = hb_ctr[0] % 4
            hb_ctr[0] += 1
            wsD.chunk2(w1_slots[r], war_val=w12_war[r])
            w12_xp[(e, hb)] = wsD.chunk2(w2_slots[r], war_val=w12_war[r])
            return r

        def emit_w3_chunks(e, dcol):
            r = w3_ctr[0] % 3
            w3_ctr[0] += 1
            for b in range(4):
                w3_xp[(e, dcol)] = wsD.chunk2(
                    w3_slots[r][:, :, b * 128:(b + 1) * 128],
                    war_val=w3_war[r])
            return r

        def prefetch_tokens(e, war_pe):
            pe = e % 2
            p.wait_t("sync", scat_done.get(e))
            for rt in route_done:
                p.wait_t("sync", rt)
            el = p.dma(eids[:, pe, 0:6],
                       elist[e * 640:e * 640 + CAP, :].rearrange(
                           "(p a) o -> p (a o)", a=3), sem="s_y")
            p.wait_t("gpsimd", el)
            for rt in route_done:
                p.wait_t("gpsimd", rt)
            for st in range(3):
                off = eids[:, pe, 2 * st:2 * st + 1].bitcast(mybir.dt.uint32)
                gx = p.gdma(lambda g, off=off: g.indirect_dma_start(
                    xe_tm[:, :], None, xn_scr,
                    bass.IndirectOffsetOnAxis(ap=off, axis=0)))
                p.wait_t("sync", gx)
                p.wait("sync", "s_pe", war_pe)
                gath_xp[e] = p.xp(xe_fm[:, :, st * 128:(st + 1) * 128],
                                  xe_tm[:, :])
                p.wait_t("gpsimd", gath_xp[e])
                gw = p.gdma(lambda g, off=off, pe=pe, st=st:
                            g.indirect_dma_start(
                                wloc[:, pe, st * 8:(st + 1) * 8], None, w_scr,
                                bass.IndirectOffsetOnAxis(ap=off, axis=0)),
                            sem="s_cc")
            p.wait_t("vector", gw)
            gath_w[e] = p.op("vector", lambda v, pe=pe, e=e: v.tensor_copy(
                wcol[:, pe, 0:3],
                wloc[:, pe, :].rearrange("p (s e) -> p s e", e=8)[:, :, e]))

        def emit_ab(e, hb, r):
            p.wait_t("tensor", w12_xp[(e, hb)])
            p.wait_t("tensor", gath_xp[e])
            for i in range(2):
                p.wait("tensor", "s_dve", ab_free[i])
                p.wait("tensor", "s_dve", yv_done[i])

            def ab_mms(t, r=r):
                last = None
                for dc in range(NDC):
                    st, sp = dc == 0, dc == NDC - 1
                    t.matmul(psAB[0][:, 0:CAP], w1_slots[r][:, dc, :],
                             xe_fm[:, dc, :], start=st, stop=sp)
                    last = t.matmul(psAB[1][:, 0:CAP], w2_slots[r][:, dc, :],
                                    xe_fm[:, dc, :], start=st, stop=sp)
                return last
            last_ab = p.op("tensor", ab_mms, inc="s_pe")
            p.wait("scalar", "s_pe", last_ab)
            p.wait("scalar", "s_dve", ab_free[0])
            av = p.op("scalar", lambda s: s.activation(
                aprime[:, 0:CAP], psAB[0][:, 0:CAP], AF.Silu), inc="s_act")
            p.wait("vector", "s_act", av)
            p.wait("vector", "s_pe", hp_war[0])

            def h_mul(v, hb=hb):
                return v.tensor_tensor(hprime[:, hb, :], psAB[1][:, 0:CAP],
                                       aprime[:, 0:CAP], ALU.mult)
            hv = p.op("vector", h_mul, inc="s_dve")
            ab_free[0] = hv
            ab_free[1] = hv
            hp_done[hb] = hv
            w12_war[r] = last_ab
            return last_ab

        def emit_y(e, dcol, r):
            pe = e % 2
            p.wait_t("tensor", w3_xp[(e, dcol)])
            for hb in range(NHC):
                p.wait("tensor", "s_dve", hp_done[hb])
            last_y = 0
            for st in range(3):
                pr = (dcol * 3 + st) % 4
                p.wait("tensor", "s_dve", yv_done[pr])

                def y_mms(t, st=st, r=r, pr=pr):
                    last = None
                    for hc in range(NHC):
                        last = t.matmul(psAB[pr],
                                        hprime[:, hc, st * 128:(st + 1) * 128],
                                        w3_slots[r][:, hc, :],
                                        start=(hc == 0), stop=(hc == NHC - 1))
                    return last
                yv = p.op("tensor", y_mms, inc="s_pe")
                last_y = yv
                p.wait("vector", "s_pe", yv)
                p.wait("vector", "s_dve", gath_w[e])
                if scat_done.get(e) is not None:
                    p.wait_t("vector", scat_done[e])

                def y_ev(v, st=st, dcol=dcol, pe=pe, pr=pr):
                    return v.tensor_scalar(
                        ybuf[:, st, dcol * 512:(dcol + 1) * 512],
                        psAB[pr], wcol[:, pe, st:st + 1], None, ALU.mult)
                yv_done[pr] = p.op("vector", y_ev, inc="s_dve")
            w3_war[r] = last_y
            return last_y

        wsD.srcs = []
        for e in range(NE):
            for hb in range(NHC):
                wsD.srcs.append((w1[e, hb * 128:(hb + 1) * 128, :], D, None))
                wsD.srcs.append((w2[e, hb * 128:(hb + 1) * 128, :], D, None))
            for dc in range(4):
                for b in range(4):
                    db = dc * 4 + b
                    wsD.srcs.append((w3[e, db * 128:(db + 1) * 128, :], HF,
                                     None))

        issue_list = []
        pos12, pos3 = [], []
        for e in range(NE):
            for hb in range(NHC):
                pos12.append(len(issue_list))
                issue_list.append(("w12", e, hb, len(pos12) - 1))
            for dc in range(4):
                pos3.append(len(issue_list))
                issue_list.append(("w3", e, dc, len(pos3) - 1))
        ip = [0]

        def ensure_issued(upto, k_emitted):
            while ip[0] <= min(upto, len(issue_list) - 1):
                kind, e_, i_, g = issue_list[ip[0]]
                if kind == "w12":
                    pred = pos12[g - 4] if g >= 4 else -1
                else:
                    pred = pos3[g - 3] if g >= 3 else -1
                if pred >= k_emitted:
                    break
                if kind == "w12":
                    emit_w12_chunks(e_, i_)
                else:
                    emit_w3_chunks(e_, i_)
                ip[0] += 1

        LA = 3
        for e in range(NE):
            if e == 0:
                ensure_issued(LA, 0)
                prefetch_tokens(0, last_c_mm)
            for hb in range(NHC):
                k = e * 15 + hb
                ensure_issued(k + LA, k)
                r = (e * NHC + hb) % 4
                la = emit_ab(e, hb, r)
            scat_done[e + 1] = scat_done.get(e)
            for dcol in range(4):
                k = e * 15 + 11 + dcol
                ensure_issued(k + LA, k)
                r = (e * 4 + dcol) % 3
                if dcol == 0 and e + 1 < NE:
                    prefetch_tokens(e + 1, la)
                ly = emit_y(e, dcol, r)
            hp_war[0] = ly
            pe = e % 2
            sd = None
            for st in range(3):
                p.wait("gpsimd", "s_dve", yv_done[(3 * 3 + st) % 4])
                p.wait("gpsimd", "s_dve", p.cnt["s_dve"])
                off = eids[:, pe, 2 * st:2 * st + 1].bitcast(mybir.dt.uint32)
                sd = p.gdma(lambda g, off=off, st=st, pe=pe:
                            g.indirect_dma_start(
                                y_scr,
                                bass.IndirectOffsetOnAxis(ap=off, axis=0),
                                ybuf[:, st, :], None,
                                compute_op=ALU.add), sem="s_cb")
            scat_done[e + 1] = sd

        p.barrier(("sync", "vector"))
        yv2 = bigD[:, 0:4096].rearrange("p (a b) -> p a b", b=D)
        for q in range(4):
            ldh = p.dma(yv2[:, 0:2, :],
                        y_scr[q * 256:(q + 1) * 256, :].rearrange(
                            "(a p) d -> p a d", p=128), sem="s_y")
            p.wait_t("vector", ldh)
            for a in range(2):
                p.op("vector", lambda v, a=a, q=q: v.tensor_tensor(
                    yb[:, q * 2 + a, :], yb[:, q * 2 + a, :], yv2[:, a, :],
                    ALU.add))
            bw = p.cnt["s_dve"]
            p.wait("sync", "s_dve", bw)

        p.barrier(("sync",))
        stv = p.dma(out.rearrange("(tc p) d -> p tc d", p=128), yb[:, :, :],
                    sem="s_y")
        p.wait_t("sync", stv)

        dstack.close()
        cstack.close()

        with nc.Block() as block:
            @block.sync
            def _(e):
                for fn in p.q["sync"]:
                    fn(e)

            @block.vector
            def _(e):
                for fn in p.q["vector"]:
                    fn(e)

            @block.scalar
            def _(e):
                for fn in p.q["scalar"]:
                    fn(e)

            @block.gpsimd
            def _(e):
                for fn in p.q["gpsimd"]:
                    fn(e)

            @block.tensor
            def _(e):
                for fn in p.q["tensor"]:
                    fn(e)

    return nc


def _rope_tables():
    half = 32
    inv = 1.0 / (10000.0 ** (np.arange(half, dtype=np.float64) / half))
    pos = np.arange(LBATCH, dtype=np.float64)
    ang = pos[:, None] * inv[None, :]
    cos64 = np.concatenate([np.cos(ang), np.cos(ang)], axis=1).astype(np.float32)
    sin64 = np.concatenate([-np.sin(ang), np.sin(ang)], axis=1).astype(np.float32)
    return cos64, sin64


def _route_consts():
    triu = np.triu(np.ones((128, 128), dtype=np.float32)).astype(bfdt)
    route = np.zeros((128, 16), dtype=np.float32)
    for e in range(8):
        route[:, e] = e * 640.0
    route[:, 8] = 512.0 + np.arange(128)
    tok = np.zeros((128, 16), dtype=np.int32)
    for tc in range(8):
        tok[:, 2 * tc] = tc * 128 + np.arange(128)
        tok[:, 2 * tc + 1] = tc * 128 + np.arange(128)
    return triu, route, tok


def kernel(L0_attn_x, L0_attn_Wq, L0_attn_Wk, L0_attn_Wv, L0_attn_Wo,
           L0_ffn_Wg, L0_ffn_W1, L0_ffn_W2, L0_ffn_W3):
    x = np.ascontiguousarray(np.asarray(L0_attn_x, dtype=np.float32))
    cos64, sin64 = _rope_tables()
    triu, route, tok = _route_consts()
    nc = build_nc()
    in_maps = []
    for c in range(8):
        b, qs = c // 4, (c % 4) * TOWN
        xr = np.ascontiguousarray(np.roll(x[b], -qs, axis=0))
        ct = np.ascontiguousarray(
            np.roll(cos64, -qs, axis=0).reshape(NTL, 128, HD).transpose(1, 0, 2)
        ).astype(bfdt)
        st = np.ascontiguousarray(
            np.roll(sin64, -qs, axis=0).reshape(NTL, 128, HD).transpose(1, 0, 2)
        ).astype(bfdt)
        in_maps.append({
            "xb": xr,
            "wq": np.ascontiguousarray(L0_attn_Wq, dtype=np.float32),
            "wk": np.ascontiguousarray(L0_attn_Wk, dtype=np.float32),
            "wv": np.ascontiguousarray(L0_attn_Wv, dtype=np.float32),
            "wo": np.ascontiguousarray(L0_attn_Wo, dtype=np.float32),
            "wg": np.ascontiguousarray(L0_ffn_Wg, dtype=np.float32),
            "w1": np.ascontiguousarray(L0_ffn_W1, dtype=np.float32),
            "w2": np.ascontiguousarray(L0_ffn_W2, dtype=np.float32),
            "w3": np.ascontiguousarray(L0_ffn_W3, dtype=np.float32),
            "cos_t": ct,
            "sin_t": st,
            "triu_c": triu,
            "route_c": route,
            "tok_c": tok,
        })
    res = run_bass_kernel_spmd(nc, in_maps, core_ids=list(range(8)))
    outf = np.empty((2, LBATCH, D), dtype=np.float32)
    for c in range(8):
        b, qs = c // 4, (c % 4) * TOWN
        outf[b, qs:qs + TOWN] = res.results[c]["out"]
    return outf


# version 15
# speedup vs baseline: 1.0358x; 1.0166x over previous
import contextlib

import numpy as np
import ml_dtypes
import concourse.bass as bass
import concourse.mybir as mybir
from concourse.bass_utils import run_bass_kernel_spmd

F32 = mybir.dt.float32
BF16 = mybir.dt.bfloat16
AF = mybir.ActivationFunctionType
ALU = mybir.AluOpType
AX = mybir.AxisListType.X
bfdt = ml_dtypes.bfloat16

D = 2048
LBATCH = 4096
TOWN = 1024
NH = 32
NKV = 8
HD = 64
NE = 8
HF = 1408
NDC = 16
NHC = 11
NTL = 32
NTO = 8

EXP_BIAS = -16.0
LN_EPS = 1e-5
RECIP_D = 1.0 / 2048.0
BIG = 1e30

ENGS = ("sync", "gpsimd", "vector", "scalar", "tensor")


DMA_SEMS = ("s_xp", "s_x0", "s_x1", "s_w0", "s_w1", "s_w2", "s_w3",
            "s_y", "s_ca", "s_cb", "s_cc")


class Pl:

    def __init__(self, sems):
        self.q = {e: [] for e in ENGS}
        self.sems = sems
        self.cnt = {k: 0 for k in sems}
        self.last_wait = {e: {k: 0 for k in sems} for e in ENGS}
        self.xp_pending = 0

    def wait(self, eng, sem, val):
        if val is None or val <= 0 or self.last_wait[eng][sem] >= val:
            return
        self.last_wait[eng][sem] = val
        h = self.sems[sem]
        self.q[eng].append(lambda e, h=h, v=val: e.wait_ge(h, v))

    def wait_t(self, eng, tok):
        if tok:
            self.wait(eng, tok[0], tok[1])

    def op(self, eng, fn, inc=None, n=1):
        if eng == "vector":
            self.wait(eng, "s_dve", self.cnt["s_dve"])
            inc, n = "s_dve", 1
        elif eng == "scalar":
            self.wait(eng, "s_act", self.cnt["s_act"])
            inc, n = "s_act", 1
        if inc is None:
            self.q[eng].append(fn)
            return None
        h = self.sems[inc]
        self.cnt[inc] += n
        self.q[eng].append(lambda e, fn=fn, h=h, n=n: fn(e).then_inc(h, n))
        return self.cnt[inc]

    def dma(self, out_ap, in_ap, sem="s_y"):
        if sem == "s_y":
            self.wait("sync", "s_y", self.cnt["s_y"])
        v = self.op("sync", lambda e, o=out_ap, i=in_ap: e.dma_start(o, i),
                    inc=sem, n=16)
        return (sem, v)

    def xp(self, out_ap, in_ap):
        self.wait("sync", "s_xp", self.cnt["s_xp"])
        v = self.op("sync", lambda e, o=out_ap, i=in_ap:
                    e.dma_start(o, i, transpose=True), inc="s_xp", n=16)
        self.xp_pending = v
        return ("s_xp", v)

    def gdma(self, fn, sem="s_ca"):
        self.wait("gpsimd", sem, self.cnt[sem])
        return (sem, self.op("gpsimd", fn, inc=sem, n=16))

    def barrier(self, engines=ENGS, exclude=()):
        tot = dict(self.cnt)
        for e in engines:
            for s, v in tot.items():
                if s in exclude:
                    continue
                self.wait(e, s, v)


class WStream:

    def __init__(self, p, buf, nslots, conv, extra_f32=(), nsem=2):
        self.p = p
        n = nslots
        flat = buf[:, :, :].rearrange("p a b -> p (a b)")
        half = n * 2048 // 2
        self.bf = [flat[:, i * 2048:(i + 1) * 2048] for i in range(n // 2)]
        f32v = flat[:, half:2 * half].bitcast(F32)
        self.f32 = [f32v[:, i * 2048:(i + 1) * 2048] for i in range(n // 4)] \
            if n >= 4 else [f32v[:, 0:1024]]
        self.f32 += list(extra_f32)
        self.nsem = nsem
        self.conv = conv
        self.csem = "s_act" if conv == "scalar" else "s_dve"
        self.i = 0
        self.conv_hist = {}
        self.xp_hist = {}
        self.srcs = None
        self.ld_hist = {}

    def _load(self, j):
        src, cols, src2 = self.srcs[j]
        fs = self.f32[j % len(self.f32)]
        dsem = "s_w%d" % (j % self.nsem)
        if src2 is None:
            ld = self.p.dma(fs[:, 0:cols], src, sem=dsem)
        else:
            self.p.dma(fs[0:64, 0:cols], src, sem=dsem)
            ld = self.p.dma(fs[64:128, 0:cols], src2, sem=dsem)
        self.ld_hist[j] = ld

    def chunk2(self, dst_ap, war_val=0):
        p = self.p
        i = self.i
        self.i += 1
        nf, nb = len(self.f32), len(self.bf)
        if i not in self.ld_hist:
            for j in range(i, min(i + nf, len(self.srcs))):
                if j - nf >= 0 and (j - nf) in self.conv_hist:
                    p.wait("sync", self.csem, self.conv_hist[j - nf])
                self._load(j)
        cols = self.srcs[i][1]
        fs, bs = self.f32[i % nf], self.bf[i % nb]
        p.wait_t(self.conv, self.ld_hist[i])
        if i >= nb:
            p.wait_t(self.conv, self.xp_hist[i - nb])
        if self.conv == "scalar":
            cv = p.op("scalar",
                      lambda s_, o=bs[:, 0:cols], i_=fs[:, 0:cols]:
                      s_.copy(o, i_), inc=self.csem)
        else:
            cv = p.op("vector",
                      lambda v, o=bs[:, 0:cols], i_=fs[:, 0:cols]:
                      v.tensor_copy(o, i_), inc=self.csem)
        self.conv_hist[i] = cv
        p.wait("sync", self.csem, cv)
        if i + nf < len(self.srcs):
            self._load(i + nf)
        p.wait("sync", "s_pe", war_val)
        xv = p.xp(dst_ap, bs[:, 0:cols])
        self.xp_hist[i] = xv
        return xv

    def chunk(self, src_ap, dst_ap, cols, war_val=0, src2=None):
        p = self.p
        i = self.i
        self.i += 1
        nf, nb = len(self.f32), len(self.bf)
        fs, bs = self.f32[i % nf], self.bf[i % nb]
        if i >= nf:
            p.wait("sync", self.csem, self.conv_hist[i - nf])
        dsem = "s_w%d" % (i % self.nsem)
        if src2 is None:
            ld = p.dma(fs[:, 0:cols], src_ap, sem=dsem)
        else:
            p.dma(fs[0:64, 0:cols], src_ap, sem=dsem)
            ld = p.dma(fs[64:128, 0:cols], src2, sem=dsem)
        p.wait_t(self.conv, ld)
        if i >= nb:
            p.wait_t(self.conv, self.xp_hist[i - nb])
        if self.conv == "scalar":
            cv = p.op("scalar", lambda s, o=bs[:, 0:cols], i_=fs[:, 0:cols]:
                      s.copy(o, i_), inc="s_act")
        else:
            cv = p.op("vector", lambda v, o=bs[:, 0:cols], i_=fs[:, 0:cols]:
                      v.tensor_copy(o, i_), inc="s_dve")
        self.conv_hist[i] = cv
        p.wait("sync", self.csem, cv)
        p.wait("sync", "s_pe", war_val)
        xv = p.xp(dst_ap, bs[:, 0:cols])
        self.xp_hist[i] = xv
        return xv


def build_nc():
    nc = bass.Bass("TRN2", target_bir_lowering=False)

    xb = nc.dram_tensor("xb", [LBATCH, D], F32, kind="ExternalInput").ap()
    wq = nc.dram_tensor("wq", [D, D], F32, kind="ExternalInput").ap()
    wk = nc.dram_tensor("wk", [512, D], F32, kind="ExternalInput").ap()
    wv = nc.dram_tensor("wv", [512, D], F32, kind="ExternalInput").ap()
    wo = nc.dram_tensor("wo", [D, D], F32, kind="ExternalInput").ap()
    wg = nc.dram_tensor("wg", [NE, D], F32, kind="ExternalInput").ap()
    w1 = nc.dram_tensor("w1", [NE, HF, D], F32, kind="ExternalInput").ap()
    w2 = nc.dram_tensor("w2", [NE, HF, D], F32, kind="ExternalInput").ap()
    w3 = nc.dram_tensor("w3", [NE, D, HF], F32, kind="ExternalInput").ap()
    cos_t = nc.dram_tensor("cos_t", [128, NTL, HD], BF16, kind="ExternalInput").ap()
    sin_t = nc.dram_tensor("sin_t", [128, NTL, HD], BF16, kind="ExternalInput").ap()
    triu_c = nc.dram_tensor("triu_c", [128, 128], BF16, kind="ExternalInput").ap()
    route_c = nc.dram_tensor("route_c", [128, 16], F32, kind="ExternalInput").ap()
    tok_c = nc.dram_tensor("tok_c", [128, 16], mybir.dt.int32, kind="ExternalInput").ap()
    out = nc.dram_tensor("out", [TOWN, D], F32, kind="ExternalOutput").ap()
    xn_scr = nc.dram_tensor("xn_scr", [1152, D], BF16, kind="Internal").ap()
    w_scr = nc.dram_tensor("w_scr", [1152, NE], F32, kind="Internal").ap()
    elist = nc.dram_tensor("elist", [NE * 640, 2], mybir.dt.int32, kind="Internal").ap()
    y_scr = nc.dram_tensor("y_scr", [1152, D], BF16, kind="Internal").ap()
    CAP = 384

    ctx = contextlib.ExitStack()
    with ctx:
        sems = {}
        for name in DMA_SEMS + ("s_pe", "s_act", "s_dve"):
            sems[name] = ctx.enter_context(nc.semaphore(name))
        p = Pl(sems)

        consts = ctx.enter_context(nc.sbuf_tensor("consts", [128, 16], F32))
        xn_own = ctx.enter_context(nc.sbuf_tensor("xn_own", [128, NDC, TOWN], BF16))
        g8 = ctx.enter_context(nc.sbuf_tensor("g8", [128, NTO, NE], F32))
        wgate = ctx.enter_context(nc.sbuf_tensor("wgate", [128, NTO, NE], F32))
        wgt = ctx.enter_context(nc.sbuf_tensor("wgt", [128, NDC, 16], BF16))
        triu_s = ctx.enter_context(nc.sbuf_tensor("triu_s", [128, 128], BF16))
        route_s = ctx.enter_context(nc.sbuf_tensor("route_s", [128, 16], F32))
        tok_s = ctx.enter_context(nc.sbuf_tensor("tok_s", [128, 16], mybir.dt.int32))
        selm = ctx.enter_context(nc.sbuf_tensor("selm", [128, NTO, NE], BF16))
        ridx = ctx.enter_context(nc.sbuf_tensor("ridx", [128, NTO, NE], F32))
        ridxu = ctx.enter_context(nc.sbuf_tensor("ridxu", [128, NTO, NE], mybir.dt.uint32))
        rtot = ctx.enter_context(nc.sbuf_tensor("rtot", [128, 80], F32))
        ones128r = ctx.enter_context(nc.sbuf_tensor("ones128r", [1, 128], F32))
        ones128c = ctx.enter_context(nc.sbuf_tensor("ones128c", [128, 1], BF16))
        pfill = ctx.enter_context(nc.sbuf_tensor("pfill", [128, 80], mybir.dt.int32))
        sA = ctx.enter_context(nc.sbuf_tensor("sA", [128, NDC, 512], BF16))
        sB = ctx.enter_context(nc.sbuf_tensor("sB", [128, NDC, 512], BF16))
        pbig = ctx.enter_context(nc.psum_tensor("pbig", [128, 4096], F32))

        def bank(i, w=512):
            return pbig[:, i * 512:i * 512 + w]

        rstack = contextlib.ExitStack()
        K_rt = rstack.enter_context(nc.sbuf_tensor("K_rt", [128, 4, LBATCH], BF16, side="right"))
        V_aug = rstack.enter_context(nc.sbuf_tensor("V_aug", [128, NTL, 520], BF16, side="right"))

        astack2 = contextlib.ExitStack()
        cos_s = astack2.enter_context(nc.sbuf_tensor("cos_s", [128, NTL, HD], BF16))
        sin_s = astack2.enter_context(nc.sbuf_tensor("sin_s", [128, NTL, HD], BF16))
        scrA = astack2.enter_context(nc.sbuf_tensor("scrA", [128, 512], F32))
        scrB = astack2.enter_context(nc.sbuf_tensor("scrB", [128, 512], F32))
        stats = astack2.enter_context(nc.sbuf_tensor("stats", [128, 8], F32))
        Q_tm = astack2.enter_context(nc.sbuf_tensor("Q_tm", [128, 2, 512], BF16))
        bigA = astack2.enter_context(nc.sbuf_tensor("bigA", [128, 4, D], BF16))
        astack1 = contextlib.ExitStack()
        xbuf = astack1.enter_context(nc.sbuf_tensor("xbuf", [128, 2, D], F32))
        xn_tm = astack1.enter_context(nc.sbuf_tensor("xn_tm", [128, 2, D], BF16))
        xnf = astack1.enter_context(nc.sbuf_tensor("xnf", [128, 3, NDC, 128], BF16))
        K_tm = astack1.enter_context(nc.sbuf_tensor("K_tm", [128, 4, 512], BF16))

        cvals = [0.125, EXP_BIAS, LN_EPS, RECIP_D, 1.0, -1.0, BIG, -BIG, 0.5]
        for i, v in enumerate(cvals):
            nc.const_aps.register(F32, v, consts[:, i:i + 1])

        def vmemsets(v):
            for i, val in enumerate(cvals):
                v.memset(consts[:, i:i + 1], val)
            return v.memset(
                V_aug[:, :, :].rearrange("p l (h c) -> p l h c", c=65)[:, :, :, 64:65],
                1.0)
        p.op("vector", vmemsets, inc="s_dve")
        for e in ("scalar", "tensor", "sync"):
            p.wait(e, "s_dve", 1)

        p.dma(cos_s[:, :, :], cos_t, sem="s_y")
        v_tab = p.dma(sin_s[:, :, :], sin_t, sem="s_y")
        p.wait_t("vector", v_tab)
        p.dma(triu_s[:, :], triu_c, sem="s_y")
        p.dma(route_s[:, :], route_c, sem="s_y")
        rc_tab = p.dma(tok_s[:, :], tok_c, sem="s_y")
        mz1 = p.op("vector", lambda v: v.memset(ones128r[0:1, :], 1.0))
        p.op("vector", lambda v: v.memset(ones128c[:, :], 1.0))
        mz2 = p.op("vector", lambda v: v.memset(pfill[:, :], 1024))
        mz3 = p.op("vector", lambda v: v.memset(xn_tm[:, 0, :], 0.0))
        mz4 = p.op("vector", lambda v: v.memset(xbuf[:, 0, :], 0.0))
        p.wait("sync", "s_dve", mz4)
        p.dma(xn_scr[1024:1152, :], xn_tm[:, 0, :], sem="s_y")
        xz16 = xbuf[:, 0, :].bitcast(BF16)
        for zi in range(9):
            p.dma(y_scr[zi * 128:(zi + 1) * 128, :], xz16[:, 0:D], sem="s_y")
        zfin = p.dma(elist.rearrange("(p a) o -> p (a o)", p=128), pfill[:, :], sem="s_y")

        wsA = WStream(p, bigA, 4, "scalar")
        for b in range(4):
            wk_ready = wsA.chunk(wk[b * 128:(b + 1) * 128, :],
                                 sA[:, :, b * 128:(b + 1) * 128], D)
        for b in range(4):
            wv_ready = wsA.chunk(wv[b * 128:(b + 1) * 128, :],
                                 sB[:, :, b * 128:(b + 1) * 128], D)
        wgslot = wsA.bf[0]
        p.wait_t("vector", wsA.xp_hist[wsA.i - 2])
        ms = p.op("vector", lambda v: v.memset(wgslot[0:16, :], 0.0), inc="s_dve")
        p.wait("sync", "s_dve", ms)
        p.wait("sync", "s_act", wsA.conv_hist[wsA.i - 1])
        ldg = p.dma(wsA.f32[0][0:8, :], wg, sem="s_w0")
        p.wait_t("scalar", ldg)
        p.wait("scalar", "s_dve", ms)
        cvg = p.op("scalar", lambda s: s.copy(wgslot[0:8, :], wsA.f32[0][0:8, :]),
                   inc="s_act")
        wsA.conv_hist[wsA.i] = cvg
        wsA.i += 1
        p.wait("sync", "s_act", cvg)
        wg_ready = p.xp(wgt[:, :, :], wgslot[0:16, :])
        wsA.xp_hist[wsA.i - 1] = wg_ready

        psV = [bank(0), bank(1)]
        psK = [bank(2), bank(3)]
        psG = [bank(4), bank(5)]

        ld_done = [0] * NTL
        sq_done = [0] * NTL
        var_done = [0] * NTL
        ln_done = [0] * NTL
        xp_done = [0] * NTL
        kv_done = [0] * NTL
        ev_done = [0] * NTL
        kxp_hist = [0] * NTL

        p.wait("sync", "s_y", zfin[1])
        p.wait("scalar", "s_y", zfin[1])
        xn_store = [0] * NTO
        for g in range(2):
            ld_done[g] = p.dma(xbuf[:, g % 2, :], xb[g * 128:(g + 1) * 128, :],
                               sem="s_x%d" % (g % 2))

        def emit_evac(gg):
            own_ = gg < NTO
            p.wait("vector", "s_pe", kv_done[gg])
            if gg >= 4:
                p.wait_t("vector", kxp_hist[gg - 4])
            pr_ = gg % 2
            ps_ = psK[pr_].rearrange("p (h c) -> p h c", c=64)
            cosb_ = cos_s[:, gg, :].unsqueeze(1).broadcast_to([128, 8, 64])
            sinb_ = sin_s[:, gg, :].unsqueeze(1).broadcast_to([128, 8, 64])
            t1_ = scrA[:, :].rearrange("p (h c) -> p h c", c=64)
            t2_ = scrB[:, :].rearrange("p (h c) -> p h c", c=64)
            p.op("vector", lambda v, g=gg, pr=pr_: v.tensor_copy(
                V_aug[:, g, :].rearrange("p (h c) -> p h c", c=65)[:, :, 0:64],
                psV[pr].rearrange("p (h c) -> p h c", c=64)))
            p.op("vector", lambda v, ps=ps_, cosb=cosb_, t1=t1_:
                 v.tensor_tensor(t1, ps, cosb, ALU.mult))
            p.op("vector", lambda v, ps=ps_, sinb=sinb_, t2=t2_:
                 v.tensor_tensor(t2[:, :, 0:32], ps[:, :, 32:64],
                                 sinb[:, :, 0:32], ALU.mult))
            p.op("vector", lambda v, ps=ps_, sinb=sinb_, t2=t2_:
                 v.tensor_tensor(t2[:, :, 32:64], ps[:, :, 0:32],
                                 sinb[:, :, 32:64], ALU.mult))
            ev_done[gg] = p.op("vector", lambda v, g=gg: v.tensor_tensor(
                K_tm[:, g % 4, :], scrA[:, :], scrB[:, :], ALU.add))
            if own_:
                ev_done[gg] = p.op("vector", lambda v, g=gg, pr=pr_: v.tensor_copy(
                    g8[:, g, :], psG[pr][:, 0:8]))
            p.wait("sync", "s_dve", ev_done[gg])
            kxp_hist[gg] = p.xp(K_rt[:, :, gg * 128:(gg + 1) * 128],
                                K_tm[:, gg % 4, :])

        for g in range(NTL):
            sl = g % 2
            own = g < NTO

            p.wait_t("scalar", ld_done[g])
            if g >= 2:
                p.wait_t("scalar", xp_done[g - 2])
                if g - 2 < NTO:
                    p.wait_t("scalar", xn_store[g - 2])
            if g >= 1:
                p.wait("scalar", "s_dve", var_done[g - 1])
            sq_done[g] = p.op("scalar", lambda s, sl=sl: s.activation(
                xn_tm[:, sl, :], xbuf[:, sl, :], AF.Square,
                accum_out=stats[:, 1:2]), inc="s_act")

            p.wait_t("vector", ld_done[g])

            p.op("vector", lambda v, sl=sl: v.reduce_sum(
                stats[:, 0:1], xbuf[:, sl, :], axis=AX))
            p.op("vector", lambda v: v.tensor_scalar(
                stats[:, 2:3], stats[:, 0:1], RECIP_D, None, ALU.mult))
            p.wait("vector", "s_act", sq_done[g])

            p.op("vector", lambda v: v.tensor_scalar(
                stats[:, 3:4], stats[:, 1:2], RECIP_D, None, ALU.mult))
            p.op("vector", lambda v: v.tensor_tensor(
                stats[:, 4:5], stats[:, 2:3], stats[:, 2:3], ALU.mult))
            var_done[g] = p.op("vector", lambda v: v.tensor_tensor(
                stats[:, 3:4], stats[:, 3:4], stats[:, 4:5], ALU.subtract))
            p.wait("scalar", "s_dve", var_done[g])
            sr = p.op("scalar", lambda s: s.activation(
                stats[:, 5:6], stats[:, 3:4], AF.Sqrt, bias=LN_EPS), inc="s_act")
            p.wait("vector", "s_act", sr)

            p.op("vector", lambda v: v.reciprocal(stats[:, 6:7], stats[:, 5:6]))
            p.op("vector", lambda v: v.tensor_scalar(
                stats[:, 7:8], stats[:, 2:3], stats[:, 6:7], -1.0,
                ALU.mult, ALU.mult))
            ln_done[g] = p.op("vector", lambda v, sl=sl: v.tensor_scalar(
                xn_tm[:, sl, :], xbuf[:, sl, :], stats[:, 6:7], stats[:, 7:8],
                ALU.mult, ALU.add))

            p.wait("sync", "s_dve", ln_done[g])
            if own:
                xn_store[g] = p.dma(xn_scr[g * 128:(g + 1) * 128, :],
                                    xn_tm[:, sl, :], sem="s_y")
                dst = xn_own[:, :, g * 128:(g + 1) * 128]
            else:
                if g >= NTO + 3:
                    p.wait("sync", "s_pe", kv_done[g - 3])
                dst = xnf[:, g % 3, :, :]
            xp_done[g] = p.xp(dst, xn_tm[:, sl, :])

            p.wait_t("tensor", xp_done[g])
            p.wait_t("tensor", wv_ready)
            p.wait_t("tensor", wg_ready)
            if g >= 2:
                p.wait("tensor", "s_dve", ev_done[g - 2])

            def kvg_mms(t, g=g, own=own):
                pr = g % 2
                last = None
                for dc in range(NDC):
                    if own:
                        lhsT = xn_own[:, dc, g * 128:(g + 1) * 128]
                    else:
                        lhsT = xnf[:, g % 3, dc, :]
                    st, sp = dc == 0, dc == NDC - 1
                    t.matmul(psV[pr], lhsT, sB[:, dc, :], start=st, stop=sp)
                    last = t.matmul(psK[pr], lhsT, sA[:, dc, :], start=st, stop=sp)
                    if own:
                        last = t.matmul(psG[pr][:, 0:8], lhsT, wgt[:, dc, 0:8],
                                        start=st, stop=sp)
                return last
            kv_done[g] = p.op("tensor", kvg_mms, inc="s_pe")

            if g >= 1:
                emit_evac(g - 1)
            if g + 2 < NTL:
                p.wait("sync", "s_dve", ln_done[g])
                p.wait("sync", "s_act", sq_done[g])
                ld_done[g + 2] = p.dma(xbuf[:, g % 2, :],
                                       xb[(g + 2) * 128:(g + 3) * 128, :],
                                       sem="s_x%d" % (g % 2))

        emit_evac(NTL - 1)

        astack1.close()
        Qp = rstack.enter_context(nc.sbuf_tensor("Qp", [128, NDC, TOWN], BF16, side="right"))
        qstg2 = astack2.enter_context(nc.sbuf_tensor("qstg2", [128, D], F32))
        a1_pe = p.cnt["s_pe"]

        gv = 0
        for tc in range(NTO):
            gg = g8[:, tc, :]
            m1, m2 = stats[:, 0:1], stats[:, 1:2]
            p.op("vector", lambda v, gg=gg, m1=m1: v.reduce_max(m1, gg, axis=AX))
            p.op("vector", lambda v, gg=gg, m1=m1, tc=tc: v.tensor_scalar(
                wgate[:, tc, :], gg, m1, -BIG, ALU.is_ge, ALU.mult))
            p.op("vector", lambda v, gg=gg, tc=tc: v.tensor_tensor(
                wgate[:, tc, :], gg, wgate[:, tc, :], ALU.add))
            p.op("vector", lambda v, m2=m2, tc=tc: v.reduce_max(
                m2, wgate[:, tc, :], axis=AX))
            p.op("vector", lambda v, gg=gg, m2=m2, tc=tc: v.tensor_scalar(
                wgate[:, tc, :], gg, m2, None, ALU.is_ge))
            p.op("vector", lambda v, m1=m1: v.tensor_scalar(
                stats[:, 2:3], m1, -1.0, None, ALU.mult))
            gv = p.op("vector", lambda v, gg=gg: v.tensor_scalar(
                gg, gg, stats[:, 2:3], None, ALU.add))
        p.wait("scalar", "s_dve", gv)

        def gate_exp(s):
            last = None
            for tc in range(NTO):
                last = s.activation(g8[:, tc, :], g8[:, tc, :], AF.Exp)
            return last
        ge = p.op("scalar", gate_exp, inc="s_act")
        p.wait("vector", "s_act", ge)

        for tc in range(NTO):
            p.op("vector", lambda v, tc=tc: v.tensor_tensor(
                g8[:, tc, :], g8[:, tc, :], wgate[:, tc, :], ALU.mult))
            p.op("vector", lambda v, tc=tc: v.reduce_sum(
                stats[:, 3:4], g8[:, tc, :], axis=AX))
            p.op("vector", lambda v: v.reciprocal(stats[:, 4:5], stats[:, 3:4]))
            p.op("vector", lambda v, tc=tc: v.tensor_scalar(
                wgate[:, tc, :], g8[:, tc, :], stats[:, 4:5], None, ALU.mult))

        psR = bank(2)
        psR2 = bank(3)
        for tc in range(NTO):
            p.op("vector", lambda v, tc=tc: v.tensor_scalar(
                selm[:, tc, :], wgate[:, tc, :], 0.0, None, ALU.is_gt))
        cum_ev = [0] * NTO
        for tc in range(NTO):
            sv = p.cnt["s_dve"]
            p.wait("tensor", "s_dve", sv)
            if tc >= 1:
                p.wait("tensor", "s_dve", cum_ev[tc - 1])
            mmv = p.op("tensor", lambda t, tc=tc: t.matmul(
                psR[:, 0:8], triu_s[:, :], selm[:, tc, :],
                start=True, stop=True), inc="s_pe")
            mmt = p.op("tensor", lambda t, tc=tc: t.matmul(
                psR2[0:1, 0:8], ones128c[:, :], selm[:, tc, :],
                start=True, stop=True), inc="s_pe")
            p.wait("vector", "s_pe", mmt)
            p.op("vector", lambda v, tc=tc: v.tensor_copy(
                ridx[:, tc, :], psR[:, 0:8]))
            cum_ev[tc] = p.op("vector", lambda v, tc=tc: v.tensor_copy(
                rtot[0:1, tc * 8:tc * 8 + 8], psR2[0:1, 0:8]))
        p.op("vector", lambda v: v.memset(rtot[0:1, 64:72], 0.0))
        for tc in range(1, NTO):
            p.op("vector", lambda v, tc=tc: v.tensor_tensor(
                rtot[0:1, 72:80], rtot[0:1, 64:72],
                rtot[0:1, (tc - 1) * 8:tc * 8], ALU.add))
            p.op("vector", lambda v: v.tensor_copy(rtot[0:1, 64:72],
                                                   rtot[0:1, 72:80]))
            ro = p.cnt["s_dve"]
            p.wait("tensor", "s_dve", ro)
            rr = p.op("tensor", lambda t, tc=tc: t.matmul(
                psR2[:, 8:16], ones128r[0:1, :], rtot[0:1, 64:72],
                start=True, stop=True), inc="s_pe")
            p.wait("vector", "s_pe", rr)
            p.op("vector", lambda v, tc=tc: v.tensor_tensor(
                ridx[:, tc, :], ridx[:, tc, :], psR2[:, 8:16], ALU.add))
        for tc in range(NTO):
            p.op("vector", lambda v, tc=tc: v.tensor_scalar(
                ridx[:, tc, :], ridx[:, tc, :], -1.0, None, ALU.add))
            p.op("vector", lambda v, tc=tc: v.tensor_tensor(
                ridx[:, tc, :], ridx[:, tc, :], selm[:, tc, :], ALU.mult))
            p.op("vector", lambda v, tc=tc: v.tensor_scalar(
                rtot[:, 0:8], selm[:, tc, :], -1.0, 1.0, ALU.mult, ALU.add))
            p.op("vector", lambda v, tc=tc: v.tensor_scalar(
                rtot[:, 0:8], rtot[:, 0:8], route_s[:, 8:9], None, ALU.mult))
            p.op("vector", lambda v, tc=tc: v.tensor_tensor(
                ridx[:, tc, :], ridx[:, tc, :], rtot[:, 0:8], ALU.add))
            p.op("vector", lambda v, tc=tc: v.tensor_tensor(
                ridx[:, tc, :], ridx[:, tc, :], route_s[:, 0:8], ALU.add))
            p.op("vector", lambda v, tc=tc: v.tensor_copy(
                ridxu[:, tc, :], ridx[:, tc, :]))
        rfin = p.cnt["s_dve"]
        p.wait("sync", "s_dve", rfin)
        p.dma(w_scr[0:TOWN, :].rearrange("(a p) e -> p a e", p=128),
              wgate[:, :, :], sem="s_y")
        p.op("vector", lambda v: v.memset(rtot[:, 0:8], 0.0))
        p.wait("sync", "s_dve", p.cnt["s_dve"])
        p.dma(w_scr[1024:1152, :], rtot[:, 0:8], sem="s_y")
        p.wait("sync", "s_y", zfin[1])
        p.wait("gpsimd", "s_y", zfin[1])
        p.wait("gpsimd", "s_dve", rfin)
        for e in range(NE):
            for tc in range(NTO):
                rt = p.gdma(lambda g, e=e, tc=tc:
                            g.indirect_dma_start(
                                elist,
                                bass.IndirectOffsetOnAxis(
                                    ap=ridxu[:, tc, e:e + 1], axis=0),
                                tok_s[:, 2 * tc:2 * tc + 2], None))
        route_done = [rt]

        psQ = [bank(0), bank(1)]
        q_ev = [0] * (4 * NTO)
        q_xp = [0] * (4 * NTO)
        qmm_by_slot = [a1_pe, a1_pe]
        p.wait("sync", "s_pe", a1_pe)
        p.wait("sync", "s_xp", p.cnt["s_xp"])
        wsA.f32.append(qstg2[:, :])
        wsA.srcs = [None] * wsA.i
        for es in range(4):
            for b in range(4):
                c = es * 4 + b
                htop = (c // 4) * 8 + (c % 4)
                wsA.srcs.append((wq[htop * 64:(htop + 1) * 64, :], D,
                                 wq[(htop + 4) * 64:(htop + 5) * 64, :]))
        for es in range(4):
            slot = sA if es % 2 == 0 else sB
            last_xp = 0
            for b in range(4):
                last_xp = wsA.chunk2(
                    slot[:, :, b * 128:(b + 1) * 128],
                    war_val=qmm_by_slot[es % 2])
            p.wait_t("tensor", last_xp)
            last_mm = 0
            for tc in range(NTO):
                fl = es * NTO + tc
                pr = fl % 2
                if fl >= 2:
                    p.wait("tensor", "s_dve", q_ev[fl - 2])
                else:
                    p.wait("tensor", "s_dve", ev_done[NTL - 1])

                def q_mms(t, tc=tc, slot=slot, pr=pr):
                    last = None
                    for dc in range(NDC):
                        last = t.matmul(psQ[pr],
                                        xn_own[:, dc, tc * 128:(tc + 1) * 128],
                                        slot[:, dc, :],
                                        start=(dc == 0), stop=(dc == NDC - 1))
                    return last
                last_mm = p.op("tensor", q_mms, inc="s_pe")

                p.wait("vector", "s_pe", last_mm)
                if fl >= 2:
                    p.wait_t("vector", q_xp[fl - 2])

                ps_ = psQ[pr].rearrange("p (h c) -> p h c", c=64)
                cosb_ = cos_s[:, tc, :].unsqueeze(1).broadcast_to([128, 8, 64])
                sinb_ = sin_s[:, tc, :].unsqueeze(1).broadcast_to([128, 8, 64])
                t1_ = scrA[:, :].rearrange("p (h c) -> p h c", c=64)
                t2_ = scrB[:, :].rearrange("p (h c) -> p h c", c=64)
                p.op("vector", lambda v, ps=ps_, cosb=cosb_, t1=t1_:
                     v.tensor_tensor(t1, ps, cosb, ALU.mult))
                p.op("vector", lambda v, ps=ps_, sinb=sinb_, t2=t2_:
                     v.tensor_tensor(t2[:, :, 0:32], ps[:, :, 32:64],
                                     sinb[:, :, 0:32], ALU.mult))
                p.op("vector", lambda v, ps=ps_, sinb=sinb_, t2=t2_:
                     v.tensor_tensor(t2[:, :, 32:64], ps[:, :, 0:32],
                                     sinb[:, :, 32:64], ALU.mult))
                q_ev[fl] = p.op("vector", lambda v, fl=fl: v.tensor_tensor(
                    Q_tm[:, fl % 2, :], scrA[:, :], scrB[:, :], ALU.add))
                p.wait("sync", "s_dve", q_ev[fl])
                q_xp[fl] = p.xp(
                    Qp[:, es * 4:(es + 1) * 4, tc * 128:(tc + 1) * 128],
                    Q_tm[:, fl % 2, :])
            qmm_by_slot[es % 2] = last_mm
        qmm_done = max(qmm_by_slot)

        p.barrier(exclude=("s_ca", "s_cb", "s_cc"))
        astack2.close()
        bstack = contextlib.ExitStack()
        attn = bstack.enter_context(nc.sbuf_tensor("attn", [128, NDC, TOWN], BF16))
        Ebuf = bstack.enter_context(nc.sbuf_tensor("Ebuf", [128, 3, 1024], BF16))
        rrow = bstack.enter_context(nc.sbuf_tensor("rrow", [128, 512], F32))
        nrm = bstack.enter_context(nc.sbuf_tensor("nrm", [128, 512], BF16))
        ones64 = bstack.enter_context(nc.sbuf_tensor("ones64", [1, 64], F32))
        ov = p.op("vector", lambda v: v.memset(ones64[0:1, :], 1.0), inc="s_dve")
        p.wait("tensor", "s_dve", ov)

        psS = [pbig[:, 0:1024], pbig[:, 1024:2048], pbig[:, 2048:3072]]
        psPV = [pbig[:, 3072:3584], pbig[:, 3584:4096]]

        GROUPS = [(2 * i, 2 * i + 2) for i in range(16)]
        exp_hist = []
        pv_hist = []
        mult_hist = []
        sBflat = sB[:, :, :].rearrange("p a b -> p (a b)")
        wo_pref = {}
        wo_war = {}

        def prefetch_wo_ec0():
            last = 0
            prev_cv = 0
            prev_xp = 0
            for b in range(4):
                f32v = sBflat[:, 4096:8192].bitcast(F32)
                if prev_cv:
                    p.wait("sync", "s_dve", prev_cv)
                ld = p.dma(f32v, wo[b * 128:(b + 1) * 128, :], sem="s_w0")
                p.wait_t("vector", ld)
                if prev_xp:
                    p.wait_t("vector", prev_xp)
                cv = p.op("vector", lambda v, o=sBflat[:, 0:2048], i_=f32v:
                          v.tensor_copy(o, i_), inc="s_dve")
                prev_cv = cv
                p.wait("sync", "s_dve", cv)
                p.wait("sync", "s_pe", qmm_by_slot[0])
                last = p.xp(sA[:, :, b * 128:(b + 1) * 128], sBflat[:, 0:2048])
                prev_xp = last
            wo_pref[0] = last

        prefetch_wo_ec0()

        rb_borrow = {0: 0, 1: 0, 2: 0}
        gabs_ctr = [0]

        def emit_s(gabs, c, st2_, lc):
            kc = c // 4
            qoff = st2_ * 512
            sl = gabs % 3
            if gabs >= 3:
                p.wait("tensor", "s_act", exp_hist[gabs - 3])
            p.wait("tensor", "s_dve", rb_borrow[sl])

            def s_mms(t, sl=sl, kc=kc, c=c, qoff=qoff, lc=lc):
                t.matmul(psS[sl][:, 0:512],
                         K_rt[0:64, kc, lc * 128:(lc + 1) * 128],
                         Qp[0:64, c, qoff:qoff + 512],
                         start=True, stop=True)
                return t.matmul(psS[sl][:, 512:1024],
                                K_rt[64:128, kc, lc * 128:(lc + 1) * 128],
                                Qp[64:128, c, qoff:qoff + 512],
                                start=True, stop=True)
            sv = p.op("tensor", s_mms, inc="s_pe")
            p.wait("scalar", "s_pe", sv)
            if gabs >= 3:
                p.wait("scalar", "s_pe", pv_hist[gabs - 3])
            ev = p.op("scalar", lambda s, sl=sl: s.activation(
                Ebuf[:, sl, :], psS[sl][:, :], AF.Exp,
                bias=EXP_BIAS, scale=0.125), inc="s_act")
            exp_hist.append(ev)

        def emit_pv(gabs, c, lc):
            kv_e = (c // 4) * 2
            sl = gabs % 3
            p.wait("tensor", "s_act", exp_hist[gabs])
            if lc == 0 and len(mult_hist) >= 2:
                p.wait("tensor", "s_dve", mult_hist[-1])

            def pv_mms(t, sl=sl, kv_e=kv_e, lc=lc):
                t.matmul(psPV[0][0:65, :],
                         V_aug[:, lc, kv_e * 65:kv_e * 65 + 65],
                         Ebuf[:, sl, 0:512],
                         start=(lc == 0), stop=(lc == NTL - 1))
                return t.matmul(psPV[1][0:65, :],
                                V_aug[:, lc, (kv_e + 1) * 65:(kv_e + 1) * 65 + 65],
                                Ebuf[:, sl, 512:1024],
                                start=(lc == 0), stop=(lc == NTL - 1))
            pv_hist.append(p.op("tensor", pv_mms, inc="s_pe"))

        for c in range(16):
            he = (c // 4) * 8 + (c % 4)
            for st2 in range(2):
                qoff = st2 * 512
                g0 = gabs_ctr[0]
                for lc in range(NTL):
                    emit_s(g0 + lc, c, st2, lc)
                    if lc >= 2:
                        emit_pv(g0 + lc - 2, c, lc - 2)
                emit_pv(g0 + NTL - 2, c, NTL - 2)
                emit_pv(g0 + NTL - 1, c, NTL - 1)
                gabs_ctr[0] = g0 + NTL

                sl_last = (g0 + NTL - 1) % 3
                p.wait("vector", "s_pe", pv_hist[-1])
                for j, h in ((0, he), (1, he + 4)):
                    ec, qrow = h // 2, 64 * (h % 2)
                    rbt = psS[sl_last][:, j * 512:(j + 1) * 512]
                    rc = p.op("vector", lambda v, j=j: v.reciprocal(
                        rrow[0:1, :], psPV[j][64:65, :]))
                    p.wait("tensor", "s_dve", rc)
                    rb = p.op("tensor", lambda t, rbt=rbt: t.matmul(
                        rbt[0:64, :], ones64[0:1, :], rrow[0:1, :],
                        start=True, stop=True), inc="s_pe")
                    p.wait("vector", "s_pe", rb)
                    rbc = p.op("vector", lambda v, rbt=rbt: v.tensor_copy(
                        rrow[0:64, :], rbt[0:64, :]))
                    rb_borrow[sl_last] = rbc
                    if qrow == 0:
                        mh = p.op("vector", lambda v, ec=ec, qoff=qoff, j=j:
                                  v.tensor_tensor(attn[0:64, ec, qoff:qoff + 512],
                                                  psPV[j][0:64, :], rrow[0:64, :],
                                                  ALU.mult))
                    else:
                        p.op("vector", lambda v, j=j: v.tensor_tensor(
                            nrm[0:64, :], psPV[j][0:64, :], rrow[0:64, :],
                            ALU.mult))
                        mh = p.op("vector", lambda v, ec=ec, qoff=qoff:
                                  v.tensor_copy(attn[64:128, ec, qoff:qoff + 512],
                                                nrm[0:64, :]))
                    mult_hist.append(mh)

        p.barrier(exclude=("s_ca", "s_cb", "s_cc"))
        rstack.close()
        cstack = contextlib.ExitStack()
        yb = cstack.enter_context(nc.sbuf_tensor("yb", [128, NTO, D], F32, side="right"))
        c2stack = contextlib.ExitStack()
        cstage = c2stack.enter_context(nc.sbuf_tensor("cstage", [128, 8, D], BF16, side="right"))
        wsC = WStream(p, cstage, 8, "scalar")
        psY = [bank(0), bank(1)]

        y_init = p.dma(yb[:, :, :],
                       xb[0:TOWN, :].rearrange("(tc p) d -> p tc d", p=128),
                       sem="s_y")

        c_ev = [0, 0]
        last_c_mm = 0
        wo_war = {0: qmm_by_slot[1], 1: qmm_by_slot[1]}
        for ec in range(4):
            slot = sA if ec % 2 == 0 else sB
            if ec >= 1:
                for b in range(4):
                    eb = ec * 4 + b
                    wo_pref[ec] = wsC.chunk(wo[eb * 128:(eb + 1) * 128, :],
                                            slot[:, :, b * 128:(b + 1) * 128], D,
                                            war_val=wo_war[ec % 2])
            p.wait_t("tensor", wo_pref[ec])
            for tc in range(NTO):
                pr = tc % 2
                p.wait("tensor", "s_dve", c_ev[pr])

                def wo_mms(t, tc=tc, slot=slot, pr=pr):
                    last = None
                    for dc in range(NDC):
                        last = t.matmul(psY[pr],
                                        attn[:, dc, tc * 128:(tc + 1) * 128],
                                        slot[:, dc, :],
                                        start=(dc == 0), stop=(dc == NDC - 1))
                    return last
                mv = p.op("tensor", wo_mms, inc="s_pe")
                last_c_mm = mv
                p.wait("vector", "s_pe", mv)
                p.wait_t("vector", y_init)
                c_ev[pr] = p.op("vector", lambda v, tc=tc, ec=ec, pr=pr:
                                v.tensor_tensor(
                                    yb[:, tc, ec * 512:(ec + 1) * 512], psY[pr],
                                    yb[:, tc, ec * 512:(ec + 1) * 512], ALU.add),
                                inc="s_dve")
            wo_war[ec % 2] = last_c_mm

        p.barrier(exclude=("s_ca", "s_cb", "s_cc"))
        bstack.close()
        c2stack.close()
        dstack = contextlib.ExitStack()
        hprime = dstack.enter_context(nc.sbuf_tensor("hprime", [128, NHC, CAP], BF16))
        aprime = dstack.enter_context(nc.sbuf_tensor("aprime", [128, 512], BF16))
        bigD = dstack.enter_context(nc.sbuf_tensor("bigD", [128, 13824], BF16))
        xe_tm = dstack.enter_context(nc.sbuf_tensor("xe_tm", [128, D], BF16))
        xe_fm = dstack.enter_context(nc.sbuf_tensor("xe_fm", [128, NDC, CAP], BF16))
        ybuf = dstack.enter_context(nc.sbuf_tensor("ybuf", [128, 3, D], BF16))
        eids = dstack.enter_context(nc.sbuf_tensor("eids", [128, 2, 8], mybir.dt.int32))
        wloc = dstack.enter_context(nc.sbuf_tensor("wloc", [128, 2, 24], F32))
        wcol = dstack.enter_context(nc.sbuf_tensor("wcol", [128, 2, 4], F32))
        wstg2 = dstack.enter_context(nc.sbuf_tensor("wstg2", [128, D], F32))
        wsD = WStream(p, bigD[:, 0:8192].rearrange("p (a b) -> p a b", b=D),
                      4, "vector", extra_f32=[wstg2[:, :]], nsem=4)

        wsl2 = xn_own[:, :, 0:512]
        w3r2 = xn_own[:, 0:NHC, 512:1024]
        psAB = [bank(0), bank(1), bank(2), bank(3)]
        psY2 = [bank(4), bank(5), bank(6), bank(7)]
        w1_slots = [sA[:, :, 0:128], sA[:, :, 128:256],
                    wsl2[:, :, 0:128], wsl2[:, :, 128:256]]
        w2_slots = [sA[:, :, 256:384], sA[:, :, 384:512],
                    wsl2[:, :, 256:384], wsl2[:, :, 384:512]]
        w3_ring1 = bigD[:, 8192:8192 + NHC * 512] \
            .rearrange("p (h c) -> p h c", c=512)
        w3_slots = [sB[:, 0:NHC, 0:512], w3_ring1, w3r2]

        hb_ctr = [0]
        w3_ctr = [0]
        w12_war = [last_c_mm] * 4
        w3_war = [last_c_mm] * 3
        hp_done = [0] * NHC
        ab_free = [0, 0, 0, 0]
        yv_done = [0, 0, 0, 0]
        hp_war = [0]
        w12_xp = {}
        w3_xp = {}
        gath_xp = {}
        gath_w = {}
        scat_done = {0: None}

        def emit_w12_chunks(e, hb):
            r = hb_ctr[0] % 4
            hb_ctr[0] += 1
            wsD.chunk2(w1_slots[r], war_val=w12_war[r])
            w12_xp[(e, hb)] = wsD.chunk2(w2_slots[r], war_val=w12_war[r])
            return r

        def emit_w3_chunks(e, dcol):
            r = w3_ctr[0] % 3
            w3_ctr[0] += 1
            for b in range(4):
                w3_xp[(e, dcol)] = wsD.chunk2(
                    w3_slots[r][:, :, b * 128:(b + 1) * 128],
                    war_val=w3_war[r])
            return r

        def prefetch_tokens(e, war_pe):
            pe = e % 2
            p.wait_t("sync", scat_done.get(e))
            for rt in route_done:
                p.wait_t("sync", rt)
            el = p.dma(eids[:, pe, 0:6],
                       elist[e * 640:e * 640 + CAP, :].rearrange(
                           "(p a) o -> p (a o)", a=3), sem="s_y")
            p.wait_t("gpsimd", el)
            for rt in route_done:
                p.wait_t("gpsimd", rt)
            for st in range(3):
                off = eids[:, pe, 2 * st:2 * st + 1].bitcast(mybir.dt.uint32)
                gx = p.gdma(lambda g, off=off: g.indirect_dma_start(
                    xe_tm[:, :], None, xn_scr,
                    bass.IndirectOffsetOnAxis(ap=off, axis=0)))
                p.wait_t("sync", gx)
                p.wait("sync", "s_pe", war_pe)
                gath_xp[e] = p.xp(xe_fm[:, :, st * 128:(st + 1) * 128],
                                  xe_tm[:, :])
                p.wait_t("gpsimd", gath_xp[e])
                gw = p.gdma(lambda g, off=off, pe=pe, st=st:
                            g.indirect_dma_start(
                                wloc[:, pe, st * 8:(st + 1) * 8], None, w_scr,
                                bass.IndirectOffsetOnAxis(ap=off, axis=0)),
                            sem="s_cc")
            p.wait_t("vector", gw)
            gath_w[e] = p.op("vector", lambda v, pe=pe, e=e: v.tensor_copy(
                wcol[:, pe, 0:3],
                wloc[:, pe, :].rearrange("p (s e) -> p s e", e=8)[:, :, e]))

        def emit_ab(e, hb, r):
            p.wait_t("tensor", w12_xp[(e, hb)])
            p.wait_t("tensor", gath_xp[e])
            for i in range(2):
                p.wait("tensor", "s_dve", ab_free[i])

            def ab_mms(t, r=r):
                last = None
                for dc in range(NDC):
                    st, sp = dc == 0, dc == NDC - 1
                    t.matmul(psAB[0][:, 0:CAP], w1_slots[r][:, dc, :],
                             xe_fm[:, dc, :], start=st, stop=sp)
                    last = t.matmul(psAB[1][:, 0:CAP], w2_slots[r][:, dc, :],
                                    xe_fm[:, dc, :], start=st, stop=sp)
                return last
            last_ab = p.op("tensor", ab_mms, inc="s_pe")
            p.wait("scalar", "s_pe", last_ab)
            p.wait("scalar", "s_dve", ab_free[0])
            av = p.op("scalar", lambda s: s.activation(
                aprime[:, 0:CAP], psAB[0][:, 0:CAP], AF.Silu), inc="s_act")
            p.wait("vector", "s_act", av)
            p.wait("vector", "s_pe", hp_war[0])

            def h_mul(v, hb=hb):
                return v.tensor_tensor(hprime[:, hb, :], psAB[1][:, 0:CAP],
                                       aprime[:, 0:CAP], ALU.mult)
            hv = p.op("vector", h_mul, inc="s_dve")
            ab_free[0] = hv
            ab_free[1] = hv
            hp_done[hb] = hv
            w12_war[r] = last_ab
            return last_ab

        def emit_y(e, dcol, r):
            pe = e % 2
            p.wait_t("tensor", w3_xp[(e, dcol)])
            for hb in range(NHC):
                p.wait("tensor", "s_dve", hp_done[hb])
            last_y = 0
            for st in range(3):
                pr = (dcol * 3 + st) % 4
                p.wait("tensor", "s_dve", yv_done[pr])

                def y_mms(t, st=st, r=r, pr=pr):
                    last = None
                    for hc in range(NHC):
                        last = t.matmul(psY2[pr],
                                        hprime[:, hc, st * 128:(st + 1) * 128],
                                        w3_slots[r][:, hc, :],
                                        start=(hc == 0), stop=(hc == NHC - 1))
                    return last
                yv = p.op("tensor", y_mms, inc="s_pe")
                last_y = yv
                p.wait("vector", "s_pe", yv)
                p.wait("vector", "s_dve", gath_w[e])
                if scat_done.get(e) is not None:
                    p.wait_t("vector", scat_done[e])

                def y_ev(v, st=st, dcol=dcol, pe=pe, pr=pr):
                    return v.tensor_scalar(
                        ybuf[:, st, dcol * 512:(dcol + 1) * 512],
                        psY2[pr], wcol[:, pe, st:st + 1], None, ALU.mult)
                yv_done[pr] = p.op("vector", y_ev, inc="s_dve")
            w3_war[r] = last_y
            return last_y

        wsD.srcs = []
        for e in range(NE):
            for hb in range(NHC):
                wsD.srcs.append((w1[e, hb * 128:(hb + 1) * 128, :], D, None))
                wsD.srcs.append((w2[e, hb * 128:(hb + 1) * 128, :], D, None))
            for dc in range(4):
                for b in range(4):
                    db = dc * 4 + b
                    wsD.srcs.append((w3[e, db * 128:(db + 1) * 128, :], HF,
                                     None))

        issue_list = []
        pos12, pos3 = [], []
        for e in range(NE):
            for hb in range(NHC):
                pos12.append(len(issue_list))
                issue_list.append(("w12", e, hb, len(pos12) - 1))
            for dc in range(4):
                pos3.append(len(issue_list))
                issue_list.append(("w3", e, dc, len(pos3) - 1))
        ip = [0]

        def ensure_issued(upto, k_emitted):
            while ip[0] <= min(upto, len(issue_list) - 1):
                kind, e_, i_, g = issue_list[ip[0]]
                if kind == "w12":
                    pred = pos12[g - 4] if g >= 4 else -1
                else:
                    pred = pos3[g - 3] if g >= 3 else -1
                if pred >= k_emitted:
                    break
                if kind == "w12":
                    emit_w12_chunks(e_, i_)
                else:
                    emit_w3_chunks(e_, i_)
                ip[0] += 1

        LA = 3
        for e in range(NE):
            if e == 0:
                ensure_issued(LA, 0)
                prefetch_tokens(0, last_c_mm)
            for hb in range(NHC):
                k = e * 15 + hb
                ensure_issued(k + LA, k)
                r = (e * NHC + hb) % 4
                la = emit_ab(e, hb, r)
            scat_done[e + 1] = scat_done.get(e)
            for dcol in range(4):
                k = e * 15 + 11 + dcol
                ensure_issued(k + LA, k)
                r = (e * 4 + dcol) % 3
                if dcol == 0 and e + 1 < NE:
                    prefetch_tokens(e + 1, la)
                ly = emit_y(e, dcol, r)
            hp_war[0] = ly
            pe = e % 2
            sd = None
            for st in range(3):
                p.wait("gpsimd", "s_dve", yv_done[(3 * 3 + st) % 4])
                p.wait("gpsimd", "s_dve", p.cnt["s_dve"])
                off = eids[:, pe, 2 * st:2 * st + 1].bitcast(mybir.dt.uint32)
                sd = p.gdma(lambda g, off=off, st=st, pe=pe:
                            g.indirect_dma_start(
                                y_scr,
                                bass.IndirectOffsetOnAxis(ap=off, axis=0),
                                ybuf[:, st, :], None,
                                compute_op=ALU.add), sem="s_cb")
            scat_done[e + 1] = sd

        p.barrier(("sync", "vector"))
        yv2 = bigD[:, 0:4096].rearrange("p (a b) -> p a b", b=D)
        for q in range(4):
            ldh = p.dma(yv2[:, 0:2, :],
                        y_scr[q * 256:(q + 1) * 256, :].rearrange(
                            "(a p) d -> p a d", p=128), sem="s_y")
            p.wait_t("vector", ldh)
            for a in range(2):
                p.op("vector", lambda v, a=a, q=q: v.tensor_tensor(
                    yb[:, q * 2 + a, :], yb[:, q * 2 + a, :], yv2[:, a, :],
                    ALU.add))
            bw = p.cnt["s_dve"]
            p.wait("sync", "s_dve", bw)

        p.barrier(("sync",))
        stv = p.dma(out.rearrange("(tc p) d -> p tc d", p=128), yb[:, :, :],
                    sem="s_y")
        p.wait_t("sync", stv)

        dstack.close()
        cstack.close()

        with nc.Block() as block:
            @block.sync
            def _(e):
                for fn in p.q["sync"]:
                    fn(e)

            @block.vector
            def _(e):
                for fn in p.q["vector"]:
                    fn(e)

            @block.scalar
            def _(e):
                for fn in p.q["scalar"]:
                    fn(e)

            @block.gpsimd
            def _(e):
                for fn in p.q["gpsimd"]:
                    fn(e)

            @block.tensor
            def _(e):
                for fn in p.q["tensor"]:
                    fn(e)

    return nc


def _rope_tables():
    half = 32
    inv = 1.0 / (10000.0 ** (np.arange(half, dtype=np.float64) / half))
    pos = np.arange(LBATCH, dtype=np.float64)
    ang = pos[:, None] * inv[None, :]
    cos64 = np.concatenate([np.cos(ang), np.cos(ang)], axis=1).astype(np.float32)
    sin64 = np.concatenate([-np.sin(ang), np.sin(ang)], axis=1).astype(np.float32)
    return cos64, sin64


def _route_consts():
    triu = np.triu(np.ones((128, 128), dtype=np.float32)).astype(bfdt)
    route = np.zeros((128, 16), dtype=np.float32)
    for e in range(8):
        route[:, e] = e * 640.0
    route[:, 8] = 512.0 + np.arange(128)
    tok = np.zeros((128, 16), dtype=np.int32)
    for tc in range(8):
        tok[:, 2 * tc] = tc * 128 + np.arange(128)
        tok[:, 2 * tc + 1] = tc * 128 + np.arange(128)
    return triu, route, tok


def kernel(L0_attn_x, L0_attn_Wq, L0_attn_Wk, L0_attn_Wv, L0_attn_Wo,
           L0_ffn_Wg, L0_ffn_W1, L0_ffn_W2, L0_ffn_W3):
    x = np.ascontiguousarray(np.asarray(L0_attn_x, dtype=np.float32))
    cos64, sin64 = _rope_tables()
    triu, route, tok = _route_consts()
    nc = build_nc()
    in_maps = []
    for c in range(8):
        b, qs = c // 4, (c % 4) * TOWN
        xr = np.ascontiguousarray(np.roll(x[b], -qs, axis=0))
        ct = np.ascontiguousarray(
            np.roll(cos64, -qs, axis=0).reshape(NTL, 128, HD).transpose(1, 0, 2)
        ).astype(bfdt)
        st = np.ascontiguousarray(
            np.roll(sin64, -qs, axis=0).reshape(NTL, 128, HD).transpose(1, 0, 2)
        ).astype(bfdt)
        in_maps.append({
            "xb": xr,
            "wq": np.ascontiguousarray(L0_attn_Wq, dtype=np.float32),
            "wk": np.ascontiguousarray(L0_attn_Wk, dtype=np.float32),
            "wv": np.ascontiguousarray(L0_attn_Wv, dtype=np.float32),
            "wo": np.ascontiguousarray(L0_attn_Wo, dtype=np.float32),
            "wg": np.ascontiguousarray(L0_ffn_Wg, dtype=np.float32),
            "w1": np.ascontiguousarray(L0_ffn_W1, dtype=np.float32),
            "w2": np.ascontiguousarray(L0_ffn_W2, dtype=np.float32),
            "w3": np.ascontiguousarray(L0_ffn_W3, dtype=np.float32),
            "cos_t": ct,
            "sin_t": st,
            "triu_c": triu,
            "route_c": route,
            "tok_c": tok,
        })
    res = run_bass_kernel_spmd(nc, in_maps, core_ids=list(range(8)))
    outf = np.empty((2, LBATCH, D), dtype=np.float32)
    for c in range(8):
        b, qs = c // 4, (c % 4) * TOWN
        outf[b, qs:qs + TOWN] = res.results[c]["out"]
    return outf
